# version 17
# baseline (speedup 1.0000x reference)
import numpy as np
import ml_dtypes
import concourse.bass as bass
import concourse.mybir as mybir
from concourse.bass_utils import run_bass_kernel_spmd

F32 = mybir.dt.float32
BF16 = mybir.dt.bfloat16
AF = mybir.ActivationFunctionType
ALU = mybir.AluOpType

T = 4096
D = 1024
TTOK = 512
NTILES = T // TTOK
Q = 128
NCH = TTOK // Q
NH, HD, DS = 32, 64, 64
DIN = 2048
CONV_DIM = 2176
DPROJ = 4256
MLPI = 4096
EPS = 1e-5

ENGS = ("sync", "tensor", "vector", "scalar", "gpsimd")
DEBUG_TAPS = {}


class Sched:
    def __init__(self):
        self.ops = []
        self.writers = {}
        self.readers = {}

    def op(self, eng, fn, reads=(), writes=(), dma=False):
        deps = set()
        for k in reads:
            deps.update(self.writers.get(k, ()))
        for k in writes:
            deps.update(self.writers.get(k, ()))
            deps.update(self.readers.get(k, ()))
        i = len(self.ops)
        for k in reads:
            self.readers.setdefault(k, []).append(i)
        for k in writes:
            if self.readers.get(k):
                self.writers[k] = [i]
            else:
                self.writers.setdefault(k, []).append(i)
                if self.writers[k][-1] != i:
                    self.writers[k].append(i)
            self.readers[k] = []
        self.ops.append({"eng": eng, "fn": fn, "dma": dma, "deps": deps,
                         "inc": dma, "sem": None, "val": None})
        return i

    def finalize(self, nc, ctx):
        for o in self.ops:
            for j in o["deps"]:
                d = self.ops[j]
                if d["eng"] != o["eng"] or d["dma"]:
                    d["inc"] = True
        self.sems = {}
        for e in ENGS:
            self.sems["e_" + e] = ctx.enter_context(nc.semaphore("e_" + e))
            self.sems["q_" + e] = ctx.enter_context(nc.semaphore("q_" + e))
        cnt = {k: 0 for k in self.sems}
        for o in self.ops:
            if o["inc"]:
                name = ("q_" if o["dma"] else "e_") + o["eng"]
                cnt[name] += 16 if o["dma"] else 1
                o["sem"] = name
                o["val"] = cnt[name]

    def replay(self, block):
        per_eng = {e: [] for e in ENGS}
        for o in self.ops:
            per_eng[o["eng"]].append(o)
        sems = self.sems
        allops = self.ops

        def make(ename):
            def body(eng):
                wm = {}
                for o in per_eng[ename]:
                    for j in sorted(o["deps"]):
                        d = allops[j]
                        if d["sem"] is None:
                            continue
                        if d["eng"] == ename and not d["dma"]:
                            continue
                        if wm.get(d["sem"], 0) >= d["val"]:
                            continue
                        eng.wait_ge(sems[d["sem"]], d["val"])
                        wm[d["sem"]] = d["val"]
                    inst = o["fn"](eng)
                    if o["inc"]:
                        inst.then_inc(sems[o["sem"]], 16 if o["dma"] else 1)
            return body
        block.sync(make("sync"))
        block.tensor(make("tensor"))
        block.vector(make("vector"))
        block.scalar(make("scalar"))
        block.gpsimd(make("gpsimd"))


def build(n_tiles=NTILES):
    nc = bass.Bass("TRN2", target_bir_lowering=False)

    x_h = nc.dram_tensor("x", [T, D], F32, kind="ExternalInput").ap()
    w_in_h = nc.dram_tensor("W_in", [D, DPROJ], F32, kind="ExternalInput").ap()
    convw_h = nc.dram_tensor("conv_w", [CONV_DIM, 4], F32, kind="ExternalInput").ap()
    convb_h = nc.dram_tensor("conv_b", [CONV_DIM], F32, kind="ExternalInput").ap()
    dtb_h = nc.dram_tensor("dt_bias", [NH], F32, kind="ExternalInput").ap()
    alog_h = nc.dram_tensor("A_log", [NH], F32, kind="ExternalInput").ap()
    dpar_h = nc.dram_tensor("D_param", [NH], F32, kind="ExternalInput").ap()
    normw_h = nc.dram_tensor("norm_w", [DIN], F32, kind="ExternalInput").ap()
    wout_h = nc.dram_tensor("W_out", [DIN, D], F32, kind="ExternalInput").ap()
    rmsw_h = nc.dram_tensor("rms_w", [D], F32, kind="ExternalInput").ap()
    w1_h = nc.dram_tensor("mlp_w1", [D, MLPI], F32, kind="ExternalInput").ap()
    b1_h = nc.dram_tensor("mlp_b1", [MLPI], F32, kind="ExternalInput").ap()
    w2_h = nc.dram_tensor("mlp_w2", [MLPI, D], F32, kind="ExternalInput").ap()
    b2_h = nc.dram_tensor("mlp_b2", [D], F32, kind="ExternalInput").ap()
    g1r_h = nc.dram_tensor("G1_R", [32, 4, 32], F32, kind="ExternalInput").ap()
    g2r_h = nc.dram_tensor("G2_R", [4, 16, 16], F32, kind="ExternalInput").ap()
    g1i_h = nc.dram_tensor("G1_I", [32, 4, 32], F32, kind="ExternalInput").ap()
    g2i_h = nc.dram_tensor("G2_I", [4, 16, 16], F32, kind="ExternalInput").ap()
    fbr_h = nc.dram_tensor("bias_real", [512], F32, kind="ExternalInput").ap()
    fbi_h = nc.dram_tensor("bias_imag", [512], F32, kind="ExternalInput").ap()
    out_h = nc.dram_tensor("out", [T, D], F32, kind="ExternalOutput").ap()

    wz_sc = nc.dram_tensor("wz_sc", [4, 128, 8 * 512], BF16).ap()
    wxbc_sc = nc.dram_tensor("wxbc_sc", [17, 128, 8, 128], BF16).ap()
    wout_sc = nc.dram_tensor("wout_sc", [4, 128, 8 * 512], BF16).ap()
    w1_sc = nc.dram_tensor("w1_sc", [32, 128, 8, 128], BF16).ap()
    w2_sc = nc.dram_tensor("w2_sc", [8, 128, 8 * 512], BF16).ap()
    wc_sc = nc.dram_tensor("wc_sc", [128, 8192], BF16).ap()
    wc2_sc = nc.dram_tensor("wc2_sc", [16, 4, 32, 512], BF16).ap()
    chl_sc = nc.dram_tensor("chl_sc", [4, 5, NH * Q], BF16).ap()

    from contextlib import ExitStack
    ctx = ExitStack()

    def sbuf(name, p_, f_, dtype):
        return ctx.enter_context(nc.sbuf_tensor(name, [p_, f_], dtype)).ap()

    s = Sched()
    V, S, TE, GP, SY = "vector", "scalar", "tensor", "gpsimd", "sync"

    with ctx:
        I64 = sbuf("I64", 64, 64, BF16)
        negones2_t = sbuf("negones2", 66, 128, BF16)
        negones2 = [negones2_t[0:2, :], negones2_t[32:34, :], negones2_t[64:66, :]]
        ones1_64_t = sbuf("ones1_64", 33, 64, BF16)
        ones1_64 = [ones1_64_t[0:1, :], ones1_64_t[32:33, :]]
        onesrow = sbuf("onesrow", 33, 128, BF16)
        ones32_64 = sbuf("ones32_64", 32, 64, BF16)
        ones_sc32 = sbuf("ones_sc32", 32, 128, F32)
        E4hl = sbuf("E4hl", 64, NH * Q, BF16)
        E1p = sbuf("E1p", 32, DIN, BF16)
        Dbc = sbuf("Dbc", 128, DIN, BF16)
        cw_sb = sbuf("cw_sb", 128, 18 * 4, F32)
        cb_sb = sbuf("cb_sb", 128, 18, F32)
        dtb_sb = sbuf("dtb_sb", 32, 1, F32)
        cpool = sbuf("cpool", 128, 8, F32)
        A_sb = sbuf("A_sb", 32, 1, F32)
        D_sb = sbuf("D_sb", 32, 1, F32)
        normw_sb = sbuf("normw_sb", 128, 16, F32)
        rmsw_sb = sbuf("rmsw_sb", 128, 8, F32)
        b1_sb = sbuf("b1_sb", 128, 32, F32)
        bias_rows = sbuf("bias_rows", 33, D, BF16)
        b2_sb = bias_rows[0:1, :]
        fb_sb = bias_rows[32:33, :]
        wdt_sb = sbuf("wdt_sb", 128, 8 * 32, BF16)

        REG_A = sbuf("REG_A", 128, 16384, BF16)
        MEa = REG_A[:, 0:4096]
        MEb = REG_A[:, 4096:8192]
        Ctld = REG_A[0:64, 8192:12288]
        flats = [REG_A[0:2, 12288:16384], REG_A[32:34, 12288:16384],
                 REG_A[64:66, 12288:16384]]
        nqhl = None
        mid_fm = REG_A[:, 0:16384]
        wslab = REG_A[:, 0:8192].bitcast(F32)
        wslab2 = REG_A[:, 8192:12288]

        REG_B = sbuf("REG_B", 128, 4096, BF16)
        xnb_tm = REG_B[:, 0:4096]
        dxs = REG_B[:, 0:2048]
        ybuf = REG_B[:, 2048:4096]
        g_tm = REG_B[:, 0:4096]
        G_rep = REG_B[:, 0:512]
        Crep = REG_B[0:64, 512:1024]
        B_tm = sbuf("b_tm", 128, 64, BF16)

        REG_C = sbuf("REG_C", 128, 8192, BF16)
        sz = REG_C[:, 0:8192]
        yg_fm = REG_C[:, 0:8192]
        g1s = REG_C[0:4, 0:2048]
        g2s = REG_C[0:4, 2048:2560]
        k2buf = REG_C.bitcast(F32)[:, 1280:1536]

        REG_D = sbuf("REG_D", 128, 8192, BF16)
        xs_tm = REG_D[:, 0:8192]
        hnb_tm = REG_D[:, 0:4096]
        hn_fm = REG_D[:, 4096:8192]

        REG_E = sbuf("REG_E", 128, 8192, BF16)
        x_tm = REG_E.bitcast(F32)[:, 0:4096]
        yg = REG_E[:, 0:8192]
        sg_tm = REG_E.bitcast(F32)[:, 0:4096]

        REG_F = sbuf("REG_F", 128, 8192, BF16)
        xn_tm = REG_F.bitcast(F32)[:, 0:4096]
        x2_tm = REG_F.bitcast(F32)[:, 0:4096]
        out_tm = REG_F.bitcast(F32)[:, 0:4096]

        h_tm_t = sbuf("h_tm", 128, 4096, F32)
        h_tm = h_tm_t[:, 0:4096]
        dtxw = h_tm_t.bitcast(BF16)[:, 0:2048]
        decrow = h_tm_t.bitcast(BF16)[0:32, 2048:4096]
        DE = h_tm_t.bitcast(BF16)[0:32, 2048:4096]
        dts3 = h_tm_t.bitcast(F32)[0:32, 1024:1152]

        SCR = sbuf("SCR", 128, 4096, BF16)
        junkS = SCR[:, 0:2048]
        junk32 = SCR.bitcast(F32)[:, 0:1024]
        junkV = SCR[:, 2048:4096]
        stats_t = sbuf("stats_t", 128, 28, F32)
        stats = stats_t[:, 0:16]
        hss = stats_t[:, 16:20]
        rr_h = stats_t[:, 20:24]
        ss_acc = stats_t[:, 24:26]
        rstd_y = stats_t[:, 26:28] if False else None
        rstd_y2 = sbuf("rstd_y2", 128, 4, F32)
        rstd_y = rstd_y2
        st_tmp = SCR.bitcast(F32)[0:64, 0:2048]

        WBUF = sbuf("WBUF", 128, 8192, BF16)
        WB = [WBUF[:, 0:4096], WBUF[:, 4096:8192]]
        ecibc = [WBUF[0:64, i * 512:(i + 1) * 512] for i in range(4)]
        nqhl = [WBUF[0:64, 4096 + i * 128: 4096 + (i + 1) * 128] for i in range(4)]

        xn_fm = sbuf("xn_fm", 128, 8 * TTOK, BF16)
        xsbc = sbuf("xsbc", 128, 18 * TTOK, BF16)
        xbcs = sbuf("xbcs", 128, TTOK + 4, BF16)
        lds = xbcs.bitcast(F32)[0:32, 0:128]
        cums = xbcs.bitcast(F32)[0:32, 128:256]
        xhist = sbuf("xhist", 128, 18 * 3, BF16)
        PAB = sbuf("PAB", 128, 2 * TTOK, BF16)
        pAs_b = PAB[:, 0:TTOK]
        pBs_b = PAB[:, TTOK:2 * TTOK]
        dt_tile = PAB.bitcast(F32)[0:32, 0:TTOK]

        S2u = sbuf("S2u", 64, DIN, F32)
        S2bu = sbuf("S2bu", 64, DIN, BF16)
        ldt_sb = sbuf("ldt_sb", 32, 128, F32)
        dtw = sbuf("dtw_t", 32, 512, BF16)
        wv = ldt_sb.bitcast(BF16)[0:32, 0:128]
        qv = lds
        mneg = ldt_sb
        chlpack = SCR[0:32, 2048:2688]
        decay = sbuf("decay", 32, 4, F32)

        PA = ctx.enter_context(nc.psum_tensor("PA", [128, 2048], F32)).ap()
        PY = ctx.enter_context(nc.psum_tensor("PY", [128, 2048], F32)).ap()

        def PAt(i):
            return PA[:, i * 512:(i + 1) * 512]

        PK = ["PA0", "PA1", "PA2", "PA3"]

        def tap(name, ap, reads, tile):
            if DEBUG_TAPS.get(name) != tile:
                return
            o = nc.dram_tensor("dbg_" + name, list(ap.shape), ap.dtype,
                               kind="ExternalOutput").ap()
            s.op(GP, lambda e, o=o, ap=ap: e.dma_start(out=o[:], in_=ap),
                 reads=reads, dma=True)

        def dma(eng, out, in_, reads=(), writes=(), **kw):
            return s.op(eng, lambda e, out=out, in_=in_, kw=kw: e.dma_start(out=out, in_=in_, **kw),
                        reads=reads, writes=writes, dma=True)

        def mems(ap, val, key):
            s.op(V, lambda e, ap=ap, val=val: e.memset(ap, val), writes=[key])

        CONSTVALS = [EPS, 1.0, -0.5, -1.0, 1.0 / DIN, 1.0 / D]
        for ci, cv in enumerate(CONSTVALS):
            mems(cpool[:, ci:ci + 1], cv, "C:imm")
            nc.const_aps.register(F32, cv, cpool[:, ci:ci + 1])
        mems(negones2[0][:], -1.0, "C:no2")
        mems(negones2[1][:], -1.0, "C:no2")
        mems(negones2[2][:], -1.0, "C:no2")
        mems(ones1_64[0][:], 1.0, "C:o64")
        mems(ones1_64[1][:], 1.0, "C:o64")
        mems(onesrow[0:1, :], 1.0, "C:orow")
        mems(onesrow[32:33, :], 1.0, "C:orow")
        mems(ones32_64[:], 1.0, "C:o32b")
        mems(ones_sc32[:], 1.0, "C:osc")
        mems(S2u[:], 0.0, "S2u")
        mems(S2bu[:], 0.0, "S2bu")
        mems(xhist[:], 0.0, "xhist")
        mems(E4hl[:], 1.0, "C:E4")
        mems(E1p[:], 1.0, "C:E1")
        mems(I64[:], 1.0, "C:I64")
        s.op(GP, lambda e: e.affine_select(
            E4hl[0:32, :], E4hl[0:32, :], pattern=[[1, 32], [0, 128]],
            compare_op=ALU.is_equal, fill=0.0, base=0, channel_multiplier=-1),
            reads=["C:E4"], writes=["C:E4"])
        s.op(GP, lambda e: e.affine_select(
            E4hl[32:64, :], E4hl[32:64, :], pattern=[[1, 32], [0, 128]],
            compare_op=ALU.is_equal, fill=0.0, base=0, channel_multiplier=-1),
            reads=["C:E4"], writes=["C:E4"])
        s.op(GP, lambda e: e.affine_select(
            E1p[:], E1p[:], pattern=[[1, 32], [0, 64]],
            compare_op=ALU.is_equal, fill=0.0, base=0, channel_multiplier=-1),
            reads=["C:E1"], writes=["C:E1"])
        s.op(GP, lambda e: e.affine_select(
            I64[:], I64[:], pattern=[[1, 64]],
            compare_op=ALU.is_equal, fill=0.0, base=0, channel_multiplier=-1),
            reads=["C:I64"], writes=["C:I64"])

        dma(GP, cw_sb[:, 0:64].rearrange("p (fc k) -> p fc k", k=4),
            convw_h[0:2048].rearrange("(fc p) k -> p fc k", p=128), writes=["C:cw"])
        dma(GP, cw_sb[0:64, 64:68], convw_h[2048:2112, :], writes=["C:cw"])
        dma(GP, cw_sb[0:64, 68:72], convw_h[2112:2176, :], writes=["C:cw"])
        dma(GP, cb_sb[:, 0:16], convb_h[0:2048].rearrange("(fc p) -> p fc", p=128), writes=["C:cb"], allow_slow_non_contiguous=True)
        dma(GP, cb_sb[0:64, 16:17], convb_h[2048:2112].unsqueeze(1), writes=["C:cb"])
        dma(GP, cb_sb[0:64, 17:18], convb_h[2112:2176].unsqueeze(1), writes=["C:cb"])
        dma(GP, dtb_sb[:], dtb_h.unsqueeze(1), writes=["C:dtb"])
        dma(GP, D_sb[:], dpar_h.unsqueeze(1), writes=["C:D"])
        dma(GP, normw_sb[:], normw_h.rearrange("(kc p) -> p kc", p=128), writes=["C:nw"], allow_slow_non_contiguous=True)
        dma(GP, rmsw_sb[:], rmsw_h.rearrange("(kc p) -> p kc", p=128), writes=["C:rw"], allow_slow_non_contiguous=True)
        dma(GP, b1_sb[:], b1_h.rearrange("(mc p) -> p mc", p=128), writes=["C:b1"], allow_slow_non_contiguous=True)
        dma(GP, b2_sb[:], b2_h.unsqueeze(0), writes=["C:b2"])
        dma(GP, fb_sb[0:1, 0:512], fbr_h.unsqueeze(0), writes=["C:fb"])
        dma(GP, fb_sb[0:1, 512:1024], fbi_h.unsqueeze(0), writes=["C:fb"])
        dma(GP, A_sb[:], alog_h.unsqueeze(1), writes=["C:A"])
        s.op(S, lambda e: e.activation(out=A_sb[:], in_=A_sb[:], func=AF.Exp),
             reads=["C:A"], writes=["C:A"])
        s.op(V, lambda e: e.tensor_scalar_mul(out=A_sb[:], in0=A_sb[:], scalar1=-1.0),
             reads=["C:A"], writes=["C:A"])
        dma(GP, wdt_sb.rearrange("p (kc m) -> p kc m", m=32),
            w_in_h[:, 4224:4256].rearrange("(kc p) m -> p kc m", p=128), writes=["C:wdt"])

        s.op(V, lambda e: e.tensor_scalar_mul(out=DE[:], in0=E1p[:], scalar1=D_sb[:]),
             reads=["C:E1", "C:D"], writes=["h"])
        for nn in range(4):
            s.op(TE, lambda e, nn=nn: e.matmul(
                PY[0:64, nn * 512:(nn + 1) * 512], ones32_64[:],
                DE[:, nn * 512:(nn + 1) * 512], start=True, stop=True),
                reads=["h", "C:o32b"], writes=["PYh0", "PYh1"])
            s.op(TE, lambda e, nn=nn: e.matmul(
                PY[64:128, nn * 512:(nn + 1) * 512], ones32_64[:],
                DE[:, nn * 512:(nn + 1) * 512], start=True, stop=True,
                tile_position=(0, 64)),
                reads=["h", "C:o32b"], writes=["PYh0", "PYh1"])
        s.op(V, lambda e: e.tensor_copy(out=Dbc[:], in_=PY[:, 0:DIN]),
             reads=["PYh0", "PYh1"], writes=["C:Dbc"])

        dma(GP, g1s[:, 0:1024].rearrange("c (b d) -> c b d", d=32),
            g1r_h.rearrange("b c d -> c b d"), writes=["C", "C:g1"])
        dma(GP, g1s[:, 1024:2048].rearrange("c (b d) -> c b d", d=32),
            g1i_h.rearrange("b c d -> c b d"), writes=["C", "C:g1"])
        dma(GP, g2s[:, 0:256], g2r_h.rearrange("c e g -> c (e g)"), writes=["C", "C:g2"])
        dma(GP, g2s[:, 256:512], g2i_h.rearrange("c e g -> c (e g)"), writes=["C", "C:g2"])
        bfR = [REG_C[0:32, 3072:3584], REG_C[0:32, 4608:5120]]
        bfI = [REG_C[0:32, 3584:4096], REG_C[0:32, 5120:5632]]
        bfnI = [REG_C[0:32, 4096:4608], REG_C[0:32, 5632:6144]]
        wcv = wc_sc.rearrange("(dd8 g2) (kc ch c) -> g2 kc dd8 ch c", g2=16, ch=2, c=512)

        def emit_wc_rearrange():
            for d8 in range(4):
                for chh in range(2):
                    for half in range(2):
                        blk = {(0, 0): 0, (0, 1): 1, (1, 0): 2, (1, 1): 3}[(half, chh)]
                        dma(GP, wcv[:, half * 4 + d8, :, chh, :].rearrange("g2 dd8 c -> dd8 g2 c"),
                            wc2_sc[:, blk, d8 * 8:(d8 + 1) * 8, :].rearrange("g2 dd8 c -> dd8 g2 c"),
                            reads=["D:wc2"], writes=["D:wc"])
        for g in range(16):
            for ri in range(2):
                pi = ri
                for b in range(32):
                    s.op(TE, lambda e, g=g, ri=ri, b=b, pi=pi: e.matmul(
                        PAt(pi)[0:32, b * 16:(b + 1) * 16],
                        g1s[:, ri * 1024 + b * 32: ri * 1024 + (b + 1) * 32],
                        g2s.rearrange("c (e g2) -> c g2 e", g2=16)[:, g + ri * 16 if False else g, :] if False else g2s[:, ri * 256:(ri + 1) * 256].rearrange("c (e g2) -> c g2 e", g2=16)[:, g, :],
                        start=True, stop=True),
                        reads=["C:g1", "C:g2"], writes=[PK[pi]])
            gb = g % 2
            s.op(V, lambda e, gb=gb: e.tensor_copy(out=bfR[gb][:], in_=PAt(0)[0:32, :]),
                 reads=["PA0"], writes=["C", f"k2buf{gb}"])
            s.op(V, lambda e, gb=gb: e.tensor_copy(out=bfI[gb][:], in_=PAt(1)[0:32, :]),
                 reads=["PA1"], writes=["C", f"k2buf{gb}"])
            s.op(V, lambda e, gb=gb: e.tensor_scalar_mul(out=bfnI[gb][:], in0=PAt(1)[0:32, :], scalar1=-1.0),
                 reads=["PA1"], writes=["C", f"k2buf{gb}"])
            dma(GP, wc2_sc[g, 0], bfR[gb][:], reads=[f"k2buf{gb}"], writes=["D:wc2"])
            dma(GP, wc2_sc[g, 1], bfI[gb][:], reads=[f"k2buf{gb}"], writes=["D:wc2"])
            dma(GP, wc2_sc[g, 2], bfnI[gb][:], reads=[f"k2buf{gb}"], writes=["D:wc2"])
            dma(GP, wc2_sc[g, 3], bfR[gb][:], reads=[f"k2buf{gb}"], writes=["D:wc2"])

        tap("E4hl", E4hl[:], ["C:E4"], 0)
        tap("E1p", E1p[:], ["C:E1"], 0)
        tap("I64", I64[:], ["C:I64"], 0)

        emit_wc_rearrange()

        for nz in range(4):
            dma(GP, wz_sc[nz].rearrange("kp (kc n) -> kp kc n", n=512),
                w_in_h[:, nz * 512:(nz + 1) * 512].rearrange("(kc kp) n -> kp kc n", kp=128),
                writes=["D:wz"])
        for mc in range(17):
            dma(GP, wxbc_sc[mc],
                w_in_h[:, 2048 + mc * 128: 2048 + (mc + 1) * 128].rearrange("(kc kp) m -> kp kc m", kp=128),
                writes=["D:wxbc"])
        for sess in range(4):
            for nd in range(2):
                dma(GP, w2_sc[nd * 4 + sess].rearrange("kp (kc n) -> kp kc n", n=512),
                    w2_h[sess * 1024:(sess + 1) * 1024, nd * 512:(nd + 1) * 512].rearrange("(kc kp) n -> kp kc n", kp=128),
                    writes=["D:w2"])
        def emit_wout_fold():
            for kc in range(16):
                dma(GP, wslab[:, 0:D], wout_h.rearrange("(kc kp) d -> kc kp d", kp=128)[kc],
                    writes=["A"])
                s.op(V, lambda e, kc=kc: e.tensor_scalar_mul(
                    out=wslab2[:, 0:D], in0=wslab[:, 0:D], scalar1=normw_sb[:, kc:kc + 1]),
                    reads=["A", "C:nw"], writes=["A"])
                sess = kc // 8
                for nd in range(2):
                    dma(GP, wout_sc[nd * 2 + sess, :, (kc % 8) * 512:(kc % 8 + 1) * 512],
                        wslab2[:, nd * 512:(nd + 1) * 512], reads=["A"], writes=["D:wout"])

        def emit_w1_fold():
            for kc in range(8):
                dma(GP, wslab[:], w1_h.rearrange("(kc kp) m -> kc kp m", kp=128)[kc],
                    writes=["A"])
                s.op(V, lambda e, kc=kc: e.tensor_scalar_mul(
                    out=wslab2[:], in0=wslab[:], scalar1=rmsw_sb[:, kc:kc + 1]),
                    reads=["A", "C:rw"], writes=["A"])
                dma(GP, w1_sc[:, :, kc, :].rearrange("mc kp m -> kp mc m"),
                    wslab2.rearrange("kp (mc m) -> kp mc m", m=128),
                    reads=["A"], writes=["D:w1"])

        for tile in range(n_tiles):
            t0 = tile * TTOK

            dma(SY, x_tm.rearrange("p (g d) -> p g d", d=D),
                x_h[t0:t0 + TTOK, :].rearrange("(g p) d -> p g d", p=128),
                writes=["E"])

            tap("x_in", x_tm[:], ["E"], tile)
            s.op(V, lambda e: e.memset(stats[:], 0.0), writes=["stats"])
            for g in range(4):
                s.op(V, lambda e, g=g: e.tensor_scalar(
                    out=junk32[:], in0=x_tm[:, g * D:(g + 1) * D], scalar1=1.0 / D,
                    scalar2=None, op0=ALU.mult, op1=ALU.add,
                    accum_out=stats[:, g:g + 1]),
                    reads=["E"], writes=["SCR", "stats"])
            for g in range(4):
                s.op(S, lambda e, g=g: e.activation(
                    out=junk32[:], in_=x_tm[:, g * D:(g + 1) * D], func=AF.Square,
                    accum_out=stats[:, 4 + g:5 + g]),
                    reads=["E"], writes=["SCR", "stats"])
            s.op(V, lambda e: e.tensor_mul(out=stats[:, 8:12], in0=stats[:, 0:4],
                                           in1=stats[:, 0:4]),
                 reads=["stats"], writes=["stats"])
            s.op(V, lambda e: e.scalar_tensor_tensor(
                out=stats[:, 8:12], in0=stats[:, 4:8], scalar=1.0 / D,
                in1=stats[:, 8:12], op0=ALU.mult, op1=ALU.subtract),
                reads=["stats", "C:imm"], writes=["stats"])
            s.op(S, lambda e: e.activation(out=stats[:, 8:12], in_=stats[:, 8:12],
                                           func=AF.Ln, bias=EPS),
                 reads=["stats", "C:imm"], writes=["stats"])
            s.op(S, lambda e: e.activation(out=stats[:, 12:16], in_=stats[:, 8:12],
                                           func=AF.Exp, scale=-0.5),
                 reads=["stats"], writes=["stats"])
            s.op(V, lambda e: e.scalar_tensor_tensor(
                out=stats[:, 0:4], in0=stats[:, 0:4], scalar=-1.0, in1=stats[:, 12:16],
                op0=ALU.mult, op1=ALU.mult),
                reads=["stats"], writes=["stats"])
            tap("stats", stats[:], ["stats"], tile)
            for g in range(4):
                s.op(S, lambda e, g=g: e.activation(
                    out=xnb_tm[:, g * D:(g + 1) * D], in_=x_tm[:, g * D:(g + 1) * D],
                    func=AF.Identity, bias=stats[:, g:g + 1], scale=stats[:, 12 + g:13 + g]),
                    reads=["E", "stats"], writes=["B"])
                s.op(V, lambda e, g=g: e.tensor_scalar(
                    out=xn_tm[:, g * D:(g + 1) * D], in0=x_tm[:, g * D:(g + 1) * D],
                    scalar1=stats[:, 12 + g:13 + g], scalar2=stats[:, g:g + 1],
                    op0=ALU.mult, op1=ALU.add),
                    reads=["E", "stats"], writes=["F"])
            for g in range(4):
                s.op(SY, lambda e, g=g: e.dma_start_transpose(
                    out=xn_fm.rearrange("p (kc t) -> p kc t", t=TTOK)[:, :, g * 128:(g + 1) * 128],
                    in_=xnb_tm[:, g * D:(g + 1) * D]),
                    reads=["B"], writes=["xnfm"], dma=True)
            tap("xn_tm", xn_tm[:], ["F"], tile)
            tap("xn_fm", xn_fm[:], ["xnfm"], tile)

            def emit_prep(c):
                b = c
                fb = c % 3
                cs = slice(c * Q, (c + 1) * Q)
                dtv = dts3[:, 0:128] if c == 3 else dt_tile[:, cs]
                dtk = "dt3" if c == 3 else "dt"
                s.op(V, lambda e, dtv=dtv: e.tensor_scalar_mul(
                    out=lds[:], in0=dtv, scalar1=A_sb[:]),
                    reads=[dtk, "C:A", "xbcs0"] + (["h"] if c == 3 else []), writes=["lds", "xbcs0"])
                s.op(V, lambda e: e.tensor_tensor_scan(
                    out=cums[:], data0=ones_sc32[:], data1=lds[:], initial=0.0,
                    op0=ALU.mult, op1=ALU.add),
                    reads=["lds", "C:osc"], writes=["cums", "xbcs0"])
                s.op(S, lambda e, dtv=dtv: e.activation(
                    out=ldt_sb[:], in_=dtv, func=AF.Ln),
                    reads=[dtk] + (["h"] if c == 3 else []), writes=["ldt"])
                s.op(V, lambda e: e.tensor_sub(out=qv[:], in0=cums[:], in1=ldt_sb[:]),
                     reads=["cums", "ldt"], writes=["lds", "xbcs0"])
                s.op(V, lambda e: e.tensor_scalar_mul(
                    out=chlpack[:, 0:128], in0=cums[:], scalar1=-1.0),
                    reads=["cums"], writes=["chp", "pAs", "SCR"])
                s.op(V, lambda e: e.tensor_scalar_mul(
                    out=mneg[:], in0=cums[:], scalar1=-1.0),
                    reads=["cums"], writes=["ldt"])
                s.op(V, lambda e: e.tensor_sub(
                    out=chlpack[:, 128:256], in0=mneg[:], in1=chlpack[:, 0:128]),
                    reads=["ldt", "chp"], writes=["chp", "pAs", "SCR"])
                s.op(V, lambda e: e.tensor_scalar_mul(
                    out=chlpack[:, 256:384], in0=qv[:], scalar1=-1.0),
                    reads=["lds"], writes=["chp", "pAs", "SCR"])
                s.op(V, lambda e: e.tensor_scalar_mul(
                    out=mneg[:], in0=qv[:], scalar1=-1.0),
                    reads=["lds"], writes=["ldt"])
                s.op(V, lambda e: e.tensor_sub(
                    out=chlpack[:, 384:512], in0=mneg[:], in1=chlpack[:, 256:384]),
                    reads=["ldt", "chp"], writes=["chp", "pAs", "SCR"])
                s.op(S, lambda e: e.activation(
                    out=chlpack[:, 512:640], in_=cums[:], func=AF.Exp),
                    reads=["cums"], writes=["chp", "pAs", "SCR"])
                s.op(S, lambda e: e.activation(
                    out=wv[:], in_=cums[:], func=AF.Exp, scale=-1.0, bias=cums[:, 127:128]),
                    reads=["cums", "ldt", "C:imm"], writes=["ldt"])
                s.op(S, lambda e, c=c: e.activation(
                    out=decay[:, c:c + 1], in_=cums[:, 127:128], func=AF.Exp),
                    reads=["cums"], writes=[f"dcy{c}"])
                s.op(V, lambda e, dtv=dtv, c=c: e.tensor_mul(
                    out=dtw[:, c * 128:(c + 1) * 128], in0=dtv, in1=wv[:]),
                    reads=[dtk, "ldt", "cums"] + (["h"] if c == 3 else []), writes=[f"dtw{c}"])
                dma(SY, chl_sc[b].rearrange("r (h i) -> h r i", i=128),
                    chlpack.rearrange("h (r i) -> h r i", i=128),
                    reads=["chp", "pAs", "SCR"], writes=[f"D:chl{b}"])
                dma(SY, flats[fb][:, 0:4096], chl_sc[b, 0:2, :],
                    reads=[f"D:chl{b}"], writes=["A", f"flat{fb}"])

            for kc in range(8):
                s.op(TE, lambda e, kc=kc: e.matmul(
                    PAt(3)[0:32, :], wdt_sb[:, kc * 32:(kc + 1) * 32],
                    xn_fm[:, kc * TTOK:(kc + 1) * TTOK],
                    start=(kc == 0), stop=(kc == 7)),
                    reads=["C:wdt", "xnfm"], writes=["PA3"])
            s.op(S, lambda e: e.activation(out=dt_tile[:], in_=PAt(3)[0:32, :],
                                           func=AF.Exp, bias=dtb_sb[:]),
                 reads=["PA3", "C:dtb", "pAs", "pBs"], writes=["dt", "pAs", "pBs"])
            s.op(S, lambda e: e.activation(out=dt_tile[:], in_=dt_tile[:],
                                           func=AF.Ln, bias=1.0),
                 reads=["dt", "C:imm"], writes=["dt", "pAs", "pBs"])
            s.op(V, lambda e: e.tensor_scalar_max(out=dt_tile[:], in0=dt_tile[:],
                                                  scalar1=1e-30),
                 reads=["dt"], writes=["dt", "pAs", "pBs"])
            tap("dt", dt_tile[:], ["dt"], tile)
            s.op(V, lambda e: e.tensor_copy(out=dts3[:], in_=dt_tile[:, 3 * Q:4 * Q]),
                 reads=["dt", "h"], writes=["dt3", "h"])

            for cpp in range(3):
                emit_prep(cpp)

            for nz in range(4):
                hb = nz % 2
                dma(SY, WB[hb], wz_sc[nz], reads=["D:wz"], writes=[f"WB{hb}"])
                for ig in range(4):
                    pi = (nz * 4 + ig) % 4
                    for kc in range(8):
                        s.op(TE, lambda e, ig=ig, kc=kc, hb=hb, pi=pi: e.matmul(
                            PAt(pi),
                            xn_fm[:, kc * TTOK + ig * 128: kc * TTOK + (ig + 1) * 128],
                            WB[hb][:, kc * 512:(kc + 1) * 512],
                            start=(kc == 0), stop=(kc == 7)),
                            reads=["xnfm", f"WB{hb}"], writes=[PK[pi]])
                    s.op(S, lambda e, ig=ig, nz=nz, pi=pi: e.activation(
                        out=sz[:, ig * DIN + nz * 512: ig * DIN + (nz + 1) * 512],
                        in_=PAt(pi), func=AF.Silu),
                        reads=[PK[pi]], writes=["C", f"sz{ig}"])
            tap("sz", sz[:], ["C"], tile)

            for mc in range(17):
                hb = (mc // 4) % 2
                if mc % 4 == 0:
                    nt_ = min(4, 17 - mc)
                    dma(SY, WB[hb][:, 0:nt_ * 1024].rearrange("p (q km) -> p q km", km=1024),
                        wxbc_sc[mc:mc + nt_].rearrange("q kp kc m -> kp q (kc m)"),
                        reads=["D:wxbc"], writes=[f"WB{hb}"])
                base = (mc % 4) * 1024
                subs = [(slice(0, 128), mc)] if mc < 16 else [(slice(0, 64), 16), (slice(0, 64), 17)]
                for si, (pr, fc) in enumerate(subs):
                    pi = (mc + si) % 4
                    for kc in range(8):
                        moff = base + (kc * 128 if mc < 16 else kc * 128 + si * 64)
                        mw = 128 if mc < 16 else 64
                        s.op(TE, lambda e, kc=kc, hb=hb, pi=pi, moff=moff, mw=mw, pr=pr: e.matmul(
                            PAt(pi)[pr, :],
                            WB[hb][:, moff: moff + mw],
                            xn_fm[:, kc * TTOK:(kc + 1) * TTOK],
                            start=(kc == 0), stop=(kc == 7)),
                            reads=[f"WB{hb}", "xnfm"], writes=[PK[pi]])
                    bb = 0
                    xb = xbcs[:, 0:TTOK + 3]
                    s.op(V, lambda e, xb=xb, pr=pr, pi=pi: e.tensor_copy(
                        out=xb[pr, 3:TTOK + 3], in_=PAt(pi)[pr, :]),
                        reads=[PK[pi]], writes=[f"xbcs{bb}"])
                    s.op(V, lambda e, xb=xb, pr=pr, fc=fc: e.tensor_copy(
                        out=xb[pr, 0:3], in_=xhist[pr, fc * 3:(fc + 1) * 3]),
                        reads=["xhist"], writes=[f"xbcs{bb}"])
                    s.op(V, lambda e, xb=xb, pr=pr, fc=fc: e.tensor_copy(
                        out=xhist[pr, fc * 3:(fc + 1) * 3], in_=xb[pr, TTOK:TTOK + 3]),
                        reads=[f"xbcs{bb}"], writes=["xhist"])
                    s.op(V, lambda e, xb=xb, pr=pr, fc=fc: e.tensor_scalar_mul(
                        out=pAs_b[pr, :], in0=xb[pr, 0:TTOK], scalar1=cw_sb[pr, fc * 4:fc * 4 + 1]),
                        reads=[f"xbcs{bb}", "C:cw"], writes=["pAs"])
                    s.op(V, lambda e, xb=xb, pr=pr, fc=fc: e.scalar_tensor_tensor(
                        out=pAs_b[pr, :], in0=xb[pr, 1:1 + TTOK], scalar=cw_sb[pr, fc * 4 + 1:fc * 4 + 2],
                        in1=pAs_b[pr, :], op0=ALU.mult, op1=ALU.add),
                        reads=[f"xbcs{bb}", "pAs"], writes=["pAs"])
                    s.op(V, lambda e, xb=xb, pr=pr, fc=fc: e.tensor_scalar_mul(
                        out=pBs_b[pr, :], in0=xb[pr, 2:2 + TTOK], scalar1=cw_sb[pr, fc * 4 + 2:fc * 4 + 3]),
                        reads=[f"xbcs{bb}", "C:cw"], writes=["pBs"])
                    s.op(V, lambda e, xb=xb, pr=pr, fc=fc: e.scalar_tensor_tensor(
                        out=pBs_b[pr, :], in0=xb[pr, 3:3 + TTOK], scalar=cw_sb[pr, fc * 4 + 3:fc * 4 + 4],
                        in1=pBs_b[pr, :], op0=ALU.mult, op1=ALU.add),
                        reads=[f"xbcs{bb}", "pBs"], writes=["pBs"])
                    s.op(V, lambda e, pr=pr: e.tensor_add(
                        out=pAs_b[pr, :], in0=pAs_b[pr, :], in1=pBs_b[pr, :]),
                        reads=["pAs", "pBs"], writes=["pAs"])
                    s.op(S, lambda e, pr=pr, fc=fc: e.activation(
                        out=xsbc[pr, fc * TTOK:(fc + 1) * TTOK], in_=pAs_b[pr, :],
                        func=AF.Silu, bias=cb_sb[pr, fc:fc + 1]),
                        reads=["pAs", "C:cb"], writes=[f"xsbc{fc}"])
            tap("xsbc", xsbc[:], [f"xsbc{i}" for i in range(18)], tile)

            for fc in range(16):
                s.op(SY, lambda e, fc=fc: e.dma_start_transpose(
                    out=xs_tm.rearrange("p (c f) -> p c f", f=DIN)[:, :, fc * 128:(fc + 1) * 128],
                    in_=xsbc[:, fc * TTOK:(fc + 1) * TTOK]),
                    reads=[f"xsbc{fc}"], writes=["D", "xstm0", "xstm1", "xstm2", "xstm3"], dma=True)

            if tile == 0:
                emit_wout_fold()
                emit_w1_fold()
            def load_nqhl(cc):
                nb2 = cc % 2
                dma(SY, nqhl[nb2][0:32, :], chl_sc[cc, 2, :].rearrange("(h i) -> h i", i=128),
                    reads=[f"D:chl{cc}"], writes=["WB1", f"nq{nb2}"])
                dma(SY, nqhl[nb2][32:64, :], chl_sc[cc, 3, :].rearrange("(h i) -> h i", i=128),
                    reads=[f"D:chl{cc}"], writes=["WB1", f"nq{nb2}"])
            load_nqhl(0)
            for c in range(NCH):
                b = c
                cs = slice(c * Q, (c + 1) * Q)
                B_fm = xsbc[0:64, 16 * TTOK + c * 128: 16 * TTOK + (c + 1) * 128]
                C_fm = xsbc[0:64, 17 * TTOK + c * 128: 17 * TTOK + (c + 1) * 128]
                xs_c = xs_tm[:, c * DIN:(c + 1) * DIN]

                nb = c % 2
                s.op(TE, lambda e, B_fm=B_fm, C_fm=C_fm: e.matmul(
                    PAt(3)[:, 0:128], B_fm, C_fm, start=True, stop=True),
                    reads=["xsbc16", "xsbc17"], writes=["PA3"])
                s.op(V, lambda e: e.tensor_copy(out=G_rep[:, 0:128], in_=PAt(3)[:, 0:128]),
                     reads=["PA3"], writes=["B", "Grep"])
                for r in range(1, 4):
                    s.op(V, lambda e, r=r: e.tensor_copy(
                        out=G_rep[:, r * 128:(r + 1) * 128], in_=G_rep[:, 0:128]),
                        reads=["Grep"], writes=["B", "Grep"])
                for r in range(4):
                    s.op(V, lambda e, r=r, C_fm=C_fm: e.tensor_copy(
                        out=Crep[:, r * 128:(r + 1) * 128], in_=C_fm),
                        reads=["xsbc17"], writes=["B", "Crep"])
                s.op(TE, lambda e, B_fm=B_fm: e.matmul(
                    PAt(3)[:, 0:64], B_fm, I64[:], start=True, stop=True),
                    reads=["xsbc16", "C:I64"], writes=["PA3"])
                s.op(V, lambda e: e.tensor_copy(out=B_tm[:], in_=PAt(3)[:, 0:64]),
                     reads=["PA3"], writes=["B", "Btm"])

                for nn in range(4):
                    s.op(TE, lambda e, nn=nn, c=c: e.matmul(
                        PY[:, nn * 512:(nn + 1) * 512], dtw[:, c * 128:(c + 1) * 128],
                        E1p[:, nn * 512:(nn + 1) * 512], start=True, stop=True),
                        reads=[f"dtw{c}", "C:E1", "PYh0" if nn < 2 else "PYh1"],
                        writes=["PYh0" if nn < 2 else "PYh1"])
                s.op(V, lambda e, xs_c=xs_c, c=c: e.tensor_mul(
                    out=dtxw[:], in0=xs_c, in1=PY[:, 0:DIN]),
                    reads=[f"xstm{c}", "PYh0", "PYh1"], writes=["h"])
                if c > 0:
                    s.op(V, lambda e, c=c: e.tensor_scalar_mul(
                        out=decrow[:], in0=E1p[:], scalar1=decay[:, c:c + 1]),
                        reads=["C:E1", f"dcy{c}"], writes=["h"])
                fb = c % 3

                def emit_ymm1(g, xs_c=xs_c, c=c):
                    for hh in range(g * 4, g * 4 + 4):
                        s.op(TE, lambda e, hh=hh, xs_c=xs_c: e.matmul(
                            PY[:, hh * HD:(hh + 1) * HD],
                            MEb[:, hh * 128:(hh + 1) * 128], xs_c[:, hh * HD:(hh + 1) * HD],
                            start=(hh % 8 == 0), stop=False),
                            reads=[f"MEb{hh // 4}", f"xstm{c}"],
                            writes=["PYh0" if hh < 16 else "PYh1", "PYy"])

                for g8 in range(8):
                    gs = slice(g8 * 512, (g8 + 1) * 512)
                    pi = g8 % 4
                    cj = g8 % 4
                    dma(SY, ecibc[cj][:],
                        chl_sc[b, 4, g8 * 512:(g8 + 1) * 512].unsqueeze(0).broadcast_to([64, 512]),
                        reads=[f"D:chl{b}"], writes=["WB0", f"ecb{cj}"])
                    s.op(TE, lambda e, gs=gs, pi=pi, fb=fb: e.matmul(
                        PAt(pi), negones2[fb][:], flats[fb][:, gs], start=True, stop=False),
                        reads=[f"flat{fb}", "C:no2"], writes=[PK[pi]])
                    s.op(TE, lambda e, gs=gs, pi=pi, nb=nb: e.matmul(
                        PAt(pi), nqhl[nb][:], E4hl[:, gs], start=False, stop=True),
                        reads=[f"nq{nb}", "C:E4"], writes=[PK[pi]])
                    s.op(S, lambda e, gs=gs, pi=pi: e.activation(
                        out=MEa[:, gs], in_=PAt(pi), func=AF.Exp),
                        reads=[PK[pi]], writes=["A", f"MEa{g8}"])
                    s.op(GP, lambda e, gs=gs: e.affine_select(
                        MEb[:, gs], MEa[:, gs], pattern=[[0, 4], [1, 128]],
                        compare_op=ALU.is_ge, fill=0.0, base=0, channel_multiplier=-1),
                        reads=[f"MEa{g8}"], writes=["A", f"MEb{g8}"])
                    s.op(V, lambda e, gs=gs: e.tensor_mul(
                        out=MEb[:, gs], in0=MEb[:, gs], in1=G_rep[:]),
                        reads=[f"MEb{g8}", "Grep"], writes=["A", f"MEb{g8}"])
                    s.op(V, lambda e, gs=gs, cj=cj: e.tensor_mul(
                        out=Ctld[:, gs], in0=Crep[:], in1=ecibc[cj][:]),
                        reads=["Crep", f"ecb{cj}"], writes=["A", f"Ctld{g8}"])
                    if g8 >= 2:
                        emit_ymm1(g8 - 2)

                if c + 3 < NCH:
                    emit_prep(c + 3)
                if c + 1 < NCH:
                    load_nqhl(c + 1)
                emit_ymm1(6)
                emit_ymm1(7)
                for hh in range(NH):
                    hs = slice(hh * HD, (hh + 1) * HD)
                    s.op(TE, lambda e, hh=hh, hs=hs: e.matmul(
                        PY[:, hs], Ctld[:, hh * 128:(hh + 1) * 128],
                        S2bu[:, hh * HD:(hh + 1) * HD], start=False, stop=True),
                        reads=[f"Ctld{hh // 4}", "S2bu", "PYy"],
                        writes=["PYh0" if hh < 16 else "PYh1"])
                tap(f"MEb_c{c}", MEb[:], [f"MEb{g} " .strip() for g in range(8)], tile)
                tap(f"flats_c{c}", flats[fb][:], [f"flat{fb}"], tile)
                tap(f"nqhl_c{c}", nqhl[nb][:], [f"nq{nb}"], tile)
                s.op(V, lambda e, xs_c=xs_c, c=c: e.tensor_mul(
                    out=dxs[:], in0=xs_c, in1=Dbc[:]),
                    reads=[f"xstm{c}", "C:Dbc"], writes=["B"])
                for hf in range(2):
                    s.op(V, lambda e, hf=hf: e.scalar_tensor_tensor(
                        out=ybuf[:, hf * 1024:(hf + 1) * 1024],
                        in0=PY[:, hf * 1024:(hf + 1) * 1024], scalar=1.0,
                        in1=dxs[:, hf * 1024:(hf + 1) * 1024],
                        op0=ALU.mult, op1=ALU.add),
                        reads=[f"PYh{hf}", "B"], writes=["B"])
                tap(f"ssd_y{c}", ybuf[:], ["B"], tile)
                for nn in range(4):
                    s.op(TE, lambda e, nn=nn: e.matmul(
                        PY[0:64, nn * 512:(nn + 1) * 512], B_tm[:],
                        dtxw[:, nn * 512:(nn + 1) * 512], start=True, stop=True),
                        reads=["Btm", "h", "PYh0" if nn < 2 else "PYh1"],
                        writes=["PYh0" if nn < 2 else "PYh1"])
                if c == 0:
                    s.op(V, lambda e, c=c: e.tensor_scalar_mul(
                        out=decrow[:], in0=E1p[:], scalar1=decay[:, c:c + 1]),
                        reads=["C:E1", f"dcy{c}"], writes=["h"])
                for nn in range(4):
                    s.op(TE, lambda e, nn=nn: e.matmul(
                        PA[0:64, nn * 512:(nn + 1) * 512], ones32_64[:],
                        decrow[:, nn * 512:(nn + 1) * 512], start=True, stop=True),
                        reads=["h", "C:o32b"], writes=[PK[nn]])
                s.op(V, lambda e: e.tensor_mul(
                    out=st_tmp[:], in0=S2u[:], in1=PA[0:64, :]),
                    reads=["S2u", "PA0", "PA1", "PA2", "PA3"], writes=["SCR"])
                s.op(V, lambda e: e.scalar_tensor_tensor(
                    out=S2u[:], in0=PY[0:64, :], scalar=1.0, in1=st_tmp[:],
                    op0=ALU.mult, op1=ALU.add),
                    reads=["PYh0", "PYh1", "SCR"], writes=["S2u"])
                s.op(V, lambda e, c=c: e.tensor_mul(
                    out=yg[:, c * DIN:(c + 1) * DIN], in0=ybuf[:],
                    in1=sz[:, c * DIN:(c + 1) * DIN]),
                    reads=["B", "C", f"sz{c}"], writes=["E", f"yg{c}"])
                for hf in range(2):
                    s.op(S, lambda e, c=c, hf=hf: e.activation(
                        out=junk32[:], in_=yg[:, c * DIN + hf * 1024: c * DIN + (hf + 1) * 1024],
                        func=AF.Square, accum_out=ss_acc[:, hf:hf + 1]),
                        reads=[f"yg{c}", "SCR"], writes=["SCR", "ssacc"])
                s.op(V, lambda e: e.tensor_add(
                    out=ss_acc[:, 0:1], in0=ss_acc[:, 0:1], in1=ss_acc[:, 1:2]),
                    reads=["ssacc"], writes=["ssacc"])
                s.op(S, lambda e: e.activation(
                    out=ss_acc[:, 0:1], in_=ss_acc[:, 0:1], func=AF.Ln, scale=1.0 / DIN, bias=EPS),
                    reads=["ssacc", "C:imm"], writes=["ssacc"])
                s.op(S, lambda e, c=c: e.activation(
                    out=rstd_y[:, c:c + 1], in_=ss_acc[:, 0:1], func=AF.Exp, scale=-0.5),
                    reads=["ssacc", "C:imm"], writes=["rstdy"])
                s.op(V, lambda e: e.tensor_copy(out=S2bu[:], in_=S2u[:]),
                     reads=["S2u"], writes=["S2bu"])

            tap("yg", yg[:], ["E"], tile)
            tap("S2u", S2u[:], ["S2u"], tile)

            for c in range(NCH):
                s.op(SY, lambda e, c=c: e.dma_start_transpose(
                    out=yg_fm.rearrange("p (fc t) -> p fc t", t=TTOK)[:, :, c * 128:(c + 1) * 128],
                    in_=yg[:, c * DIN:(c + 1) * DIN]),
                    reads=[f"yg{c}"], writes=["C"], dma=True)
            tap("yg_fm", yg_fm[:], ["C"], tile)
            tap("rstdy", rstd_y[:], ["rstdy"], tile)
            for nd in range(2):
                for ig in range(4):
                    pi = ig
                    for sess in range(2):
                        hb = sess
                        if ig == 0:
                            dma(SY, WB[hb], wout_sc[nd * 2 + sess],
                                reads=["D:wout"], writes=[f"WB{hb}"])
                        for kc in range(8):
                            kcg = sess * 8 + kc
                            s.op(TE, lambda e, ig=ig, kc=kc, hb=hb, pi=pi, kcg=kcg: e.matmul(
                                PAt(pi),
                                yg_fm[:, kcg * TTOK + ig * 128: kcg * TTOK + (ig + 1) * 128],
                                WB[hb][:, kc * 512:(kc + 1) * 512],
                                start=(kcg == 0), stop=(kcg == 15)),
                                reads=["C", f"WB{hb}"], writes=[PK[pi]])
                    s.op(V, lambda e, ig=ig, nd=nd, pi=pi: e.scalar_tensor_tensor(
                        out=h_tm[:, ig * D + nd * 512: ig * D + (nd + 1) * 512],
                        in0=PAt(pi), scalar=rstd_y[:, ig:ig + 1],
                        in1=xn_tm[:, ig * D + nd * 512: ig * D + (nd + 1) * 512],
                        op0=ALU.mult, op1=ALU.add),
                        reads=[PK[pi], "rstdy", "F"], writes=["h"])
            tap("h_tm", h_tm[:], ["h"], tile)

            s.op(V, lambda e: e.memset(hss[:], 0.0), writes=["hss"])
            for g in range(4):
                s.op(S, lambda e, g=g: e.activation(
                    out=junk32[:], in_=h_tm[:, g * D:(g + 1) * D], func=AF.Square,
                    accum_out=hss[:, g:g + 1]),
                    reads=["h", "hss"], writes=["SCR", "hss"])
            s.op(S, lambda e: e.activation(out=hss[:], in_=hss[:], func=AF.Ln,
                                           scale=1.0 / D, bias=EPS),
                 reads=["hss", "C:imm"], writes=["hss"])
            s.op(S, lambda e: e.activation(out=rr_h[:], in_=hss[:], func=AF.Exp,
                                           scale=-0.5),
                 reads=["hss", "C:imm"], writes=["rrh"])
            for g in range(4):
                s.op(V, lambda e, g=g: e.tensor_scalar_mul(
                    out=hnb_tm[:, g * D:(g + 1) * D], in0=h_tm[:, g * D:(g + 1) * D],
                    scalar1=rr_h[:, g:g + 1]),
                    reads=["h", "rrh"], writes=["D", "hnb"])
            for g in range(4):
                s.op(SY, lambda e, g=g: e.dma_start_transpose(
                    out=hn_fm.rearrange("p (kc t) -> p kc t", t=TTOK)[:, :, g * 128:(g + 1) * 128],
                    in_=hnb_tm[:, g * D:(g + 1) * D]),
                    reads=["hnb"], writes=["D", "hnfm"], dma=True)

            for mg in range(8):
                hb = mg % 2
                dma(SY, WB[hb][:, 0:4096].rearrange("p (q km) -> p q km", q=4),
                    w1_sc[mg * 4:(mg + 1) * 4].rearrange("q kp kc m -> kp q (kc m)"),
                    reads=["D:w1"], writes=[f"WB{hb}"])
                for mq in range(4):
                    mc = mg * 4 + mq
                    pi = mc % 4
                    for kc in range(8):
                        s.op(TE, lambda e, mq=mq, kc=kc, hb=hb, pi=pi: e.matmul(
                            PAt(pi),
                            WB[hb][:, mq * 1024 + kc * 128: mq * 1024 + (kc + 1) * 128],
                            hn_fm[:, kc * TTOK:(kc + 1) * TTOK],
                            start=(kc == 0), stop=(kc == 7)),
                            reads=[f"WB{hb}", "hnfm"], writes=[PK[pi]])
                    s.op(S, lambda e, mc=mc, pi=pi: e.activation(
                        out=mid_fm[:, mc * TTOK:(mc + 1) * TTOK], in_=PAt(pi),
                        func=AF.Silu, bias=b1_sb[:, mc:mc + 1]),
                        reads=[PK[pi], "C:b1"], writes=["A", f"mid{mc}"])
            for nd in range(2):
                for ig in range(4):
                    s.op(TE, lambda e, ig=ig, nd=nd: e.matmul(
                        PAt(ig), onesrow[0:1, :],
                        b2_sb[:, nd * 512:(nd + 1) * 512], start=True, stop=False),
                        reads=["C:b2", "C:orow"], writes=[PK[ig]])
                for sess in range(4):
                    hb = sess % 2
                    dma(SY, WB[hb][:, 0:4096], w2_sc[nd * 4 + sess],
                        reads=["D:w2"], writes=[f"WB{hb}"])
                    for ig in range(4):
                        for kc in range(8):
                            kcg = sess * 8 + kc
                            s.op(TE, lambda e, kc=kc, hb=hb, ig=ig, kcg=kcg: e.matmul(
                                PAt(ig),
                                mid_fm[:, kcg * TTOK + ig * 128: kcg * TTOK + (ig + 1) * 128],
                                WB[hb][:, kc * 512:(kc + 1) * 512],
                                start=False, stop=(kcg == 31)),
                                reads=[f"mid{kcg}", f"WB{hb}"], writes=[PK[ig]])
                for ig in range(4):
                    s.op(V, lambda e, ig=ig, nd=nd: e.scalar_tensor_tensor(
                        out=sg_tm[:, ig * D + nd * 512: ig * D + (nd + 1) * 512],
                        in0=PAt(ig), scalar=1.0,
                        in1=h_tm[:, ig * D + nd * 512: ig * D + (nd + 1) * 512],
                        op0=ALU.mult, op1=ALU.add),
                        reads=[PK[ig], "h"], writes=["E", "sgk"])
            tap("s_attn", sg_tm[:], ["sgk"], tile)

            for nd2 in range(2):
                hb = nd2
                dma(SY, WB[hb].rearrange("p (kc n) -> p kc n", n=512),
                    wc_sc.rearrange("kp (kc ch n) -> kp kc ch n", ch=2, n=512)[:, :, nd2, :],
                    reads=["D:wc"], writes=[f"WB{hb}"])
                for ig in range(4):
                    pi = (nd2 * 4 + ig) % 4
                    s.op(TE, lambda e, ig=ig, nd2=nd2, pi=pi: e.matmul(
                        PAt(pi), onesrow[32:33, :],
                        fb_sb[:, nd2 * 512:(nd2 + 1) * 512], start=True, stop=False),
                        reads=["C:fb", "C:orow"], writes=[PK[pi]])
                    for kc in range(8):
                        s.op(TE, lambda e, ig=ig, nd2=nd2, kc=kc, pi=pi, hb=hb: e.matmul(
                            PAt(pi),
                            xn_fm[:, kc * TTOK + ig * 128: kc * TTOK + (ig + 1) * 128],
                            WB[hb][:, kc * 512:(kc + 1) * 512],
                            start=False, stop=(kc == 7)),
                            reads=["xnfm", f"WB{hb}"], writes=[PK[pi]])
                    s.op(S, lambda e, ig=ig, nd2=nd2, pi=pi: e.activation(
                        out=g_tm[:, ig * D + nd2 * 512: ig * D + (nd2 + 1) * 512],
                        in_=PAt(pi), func=AF.Silu),
                        reads=[PK[pi]], writes=["B", "gk"])
            tap("g_tm", g_tm[:], ["gk"], tile)
            dma(SY, x2_tm.rearrange("p (g d) -> p g d", d=D),
                x_h[t0:t0 + TTOK, :].rearrange("(g p) d -> p g d", p=128),
                writes=["F"])
            for g in range(4):
                s.op(V, lambda e, g=g: e.tensor_mul(
                    out=sg_tm[:, g * D:(g + 1) * D], in0=sg_tm[:, g * D:(g + 1) * D],
                    in1=g_tm[:, g * D:(g + 1) * D]),
                    reads=["sgk", "gk", "B", "E"], writes=["E", "sgk"])
                s.op(V, lambda e, g=g: e.tensor_add(
                    out=out_tm[:, g * D:(g + 1) * D], in0=x2_tm[:, g * D:(g + 1) * D],
                    in1=sg_tm[:, g * D:(g + 1) * D]),
                    reads=["F", "sgk"], writes=["F", "outb"])
            dma(GP, out_h[t0:t0 + TTOK, :].rearrange("(g p) d -> p g d", p=128),
                out_tm.rearrange("p (g d) -> p g d", d=D),
                reads=["outb", "F"], writes=["outq"])

        s.op(GP, lambda e: e.engine_nop(), reads=["outq"])

        s.finalize(nc, ctx)
        with nc.Block() as block:
            s.replay(block)

    return nc


def _shard_inputs(inputs):
    maps = []
    f32 = lambda a: np.ascontiguousarray(np.asarray(a, dtype=np.float32))
    for b in range(8):
        maps.append({
            "x": f32(np.asarray(inputs["L0_attn_hidden_tokens"])[b]),
            "W_in": f32(inputs["L0_attn_W_in"]),
            "conv_w": f32(inputs["L0_attn_conv_w"]),
            "conv_b": f32(inputs["L0_attn_conv_b"]),
            "dt_bias": f32(inputs["L0_attn_dt_bias"]),
            "A_log": f32(inputs["L0_attn_A_log"]),
            "D_param": f32(inputs["L0_attn_D_param"]),
            "norm_w": f32(inputs["L0_attn_norm_w"]),
            "W_out": f32(inputs["L0_attn_W_out"]),
            "rms_w": f32(inputs["L0_attn_rms_w"]),
            "mlp_w1": f32(inputs["L0_attn_mlp_w1"]),
            "mlp_b1": f32(inputs["L0_attn_mlp_b1"]),
            "mlp_w2": f32(inputs["L0_attn_mlp_w2"]),
            "mlp_b2": f32(inputs["L0_attn_mlp_b2"]),
            "G1_R": f32(np.asarray(inputs["L0_ffn_G1_R"])[0]),
            "G2_R": f32(np.asarray(inputs["L0_ffn_G2_R"])[:, :, 0, :]),
            "G1_I": f32(np.asarray(inputs["L0_ffn_G1_I"])[0]),
            "G2_I": f32(np.asarray(inputs["L0_ffn_G2_I"])[:, :, 0, :]),
            "bias_real": f32(inputs["L0_ffn_bias_real"]),
            "bias_imag": f32(inputs["L0_ffn_bias_imag"]),
        })
    return maps


def kernel(**inputs):
    nc = build()
    in_maps = _shard_inputs(inputs)
    res = run_bass_kernel_spmd(nc, in_maps, core_ids=list(range(8)))
    return np.stack([r["out"] for r in res.results], axis=0)


# version 18
# speedup vs baseline: 1.0356x; 1.0356x over previous
import numpy as np
import ml_dtypes
import concourse.bass as bass
import concourse.mybir as mybir
from concourse.bass_utils import run_bass_kernel_spmd

F32 = mybir.dt.float32
BF16 = mybir.dt.bfloat16
AF = mybir.ActivationFunctionType
ALU = mybir.AluOpType

T = 4096
D = 1024
TTOK = 512
NTILES = T // TTOK
Q = 128
NCH = TTOK // Q
NH, HD, DS = 32, 64, 64
DIN = 2048
CONV_DIM = 2176
DPROJ = 4256
MLPI = 4096
EPS = 1e-5

ENGS = ("sync", "tensor", "vector", "scalar", "gpsimd")
DEBUG_TAPS = {}


class Sched:
    def __init__(self):
        self.ops = []
        self.writers = {}
        self.readers = {}

    def op(self, eng, fn, reads=(), writes=(), dma=False):
        deps = set()
        for k in reads:
            deps.update(self.writers.get(k, ()))
        for k in writes:
            deps.update(self.writers.get(k, ()))
            deps.update(self.readers.get(k, ()))
        i = len(self.ops)
        for k in reads:
            self.readers.setdefault(k, []).append(i)
        for k in writes:
            if self.readers.get(k):
                self.writers[k] = [i]
            else:
                self.writers.setdefault(k, []).append(i)
                if self.writers[k][-1] != i:
                    self.writers[k].append(i)
            self.readers[k] = []
        self.ops.append({"eng": eng, "fn": fn, "dma": dma, "deps": deps,
                         "inc": dma, "sem": None, "val": None})
        return i

    def finalize(self, nc, ctx):
        for o in self.ops:
            for j in o["deps"]:
                d = self.ops[j]
                if d["eng"] != o["eng"] or d["dma"]:
                    d["inc"] = True
        self.sems = {}
        for e in ENGS:
            self.sems["e_" + e] = ctx.enter_context(nc.semaphore("e_" + e))
            self.sems["q_" + e] = ctx.enter_context(nc.semaphore("q_" + e))
        cnt = {k: 0 for k in self.sems}
        for o in self.ops:
            if o["inc"]:
                name = ("q_" if o["dma"] else "e_") + o["eng"]
                cnt[name] += 16 if o["dma"] else 1
                o["sem"] = name
                o["val"] = cnt[name]

    def replay(self, block):
        per_eng = {e: [] for e in ENGS}
        for o in self.ops:
            per_eng[o["eng"]].append(o)
        sems = self.sems
        allops = self.ops

        def make(ename):
            def body(eng):
                wm = {}
                for o in per_eng[ename]:
                    for j in sorted(o["deps"]):
                        d = allops[j]
                        if d["sem"] is None:
                            continue
                        if d["eng"] == ename and not d["dma"]:
                            continue
                        if wm.get(d["sem"], 0) >= d["val"]:
                            continue
                        eng.wait_ge(sems[d["sem"]], d["val"])
                        wm[d["sem"]] = d["val"]
                    inst = o["fn"](eng)
                    if o["inc"]:
                        inst.then_inc(sems[o["sem"]], 16 if o["dma"] else 1)
            return body
        block.sync(make("sync"))
        block.tensor(make("tensor"))
        block.vector(make("vector"))
        block.scalar(make("scalar"))
        block.gpsimd(make("gpsimd"))


def build(n_tiles=NTILES):
    nc = bass.Bass("TRN2", target_bir_lowering=False)

    x_h = nc.dram_tensor("x", [T, D], F32, kind="ExternalInput").ap()
    w_in_h = nc.dram_tensor("W_in", [D, DPROJ], F32, kind="ExternalInput").ap()
    convw_h = nc.dram_tensor("conv_w", [CONV_DIM, 4], F32, kind="ExternalInput").ap()
    convb_h = nc.dram_tensor("conv_b", [CONV_DIM], F32, kind="ExternalInput").ap()
    dtb_h = nc.dram_tensor("dt_bias", [NH], F32, kind="ExternalInput").ap()
    alog_h = nc.dram_tensor("A_log", [NH], F32, kind="ExternalInput").ap()
    dpar_h = nc.dram_tensor("D_param", [NH], F32, kind="ExternalInput").ap()
    normw_h = nc.dram_tensor("norm_w", [DIN], F32, kind="ExternalInput").ap()
    wout_h = nc.dram_tensor("W_out", [DIN, D], F32, kind="ExternalInput").ap()
    rmsw_h = nc.dram_tensor("rms_w", [D], F32, kind="ExternalInput").ap()
    w1_h = nc.dram_tensor("mlp_w1", [D, MLPI], F32, kind="ExternalInput").ap()
    b1_h = nc.dram_tensor("mlp_b1", [MLPI], F32, kind="ExternalInput").ap()
    w2_h = nc.dram_tensor("mlp_w2", [MLPI, D], F32, kind="ExternalInput").ap()
    b2_h = nc.dram_tensor("mlp_b2", [D], F32, kind="ExternalInput").ap()
    g1r_h = nc.dram_tensor("G1_R", [32, 4, 32], F32, kind="ExternalInput").ap()
    g2r_h = nc.dram_tensor("G2_R", [4, 16, 16], F32, kind="ExternalInput").ap()
    g1i_h = nc.dram_tensor("G1_I", [32, 4, 32], F32, kind="ExternalInput").ap()
    g2i_h = nc.dram_tensor("G2_I", [4, 16, 16], F32, kind="ExternalInput").ap()
    fbr_h = nc.dram_tensor("bias_real", [512], F32, kind="ExternalInput").ap()
    fbi_h = nc.dram_tensor("bias_imag", [512], F32, kind="ExternalInput").ap()
    out_h = nc.dram_tensor("out", [T, D], F32, kind="ExternalOutput").ap()

    wz_sc = nc.dram_tensor("wz_sc", [4, 128, 8 * 512], BF16).ap()
    wxbc_sc = nc.dram_tensor("wxbc_sc", [17, 128, 8, 128], BF16).ap()
    wout_sc = nc.dram_tensor("wout_sc", [4, 128, 8 * 512], BF16).ap()
    w1_sc = nc.dram_tensor("w1_sc", [32, 128, 8, 128], BF16).ap()
    w2_sc = nc.dram_tensor("w2_sc", [8, 128, 8 * 512], BF16).ap()
    wc_sc = nc.dram_tensor("wc_sc", [128, 8192], BF16).ap()
    wc2_sc = nc.dram_tensor("wc2_sc", [16, 4, 32, 512], BF16).ap()
    chl_sc = nc.dram_tensor("chl_sc", [4, 5, NH * Q], BF16).ap()

    from contextlib import ExitStack
    ctx = ExitStack()

    def sbuf(name, p_, f_, dtype):
        return ctx.enter_context(nc.sbuf_tensor(name, [p_, f_], dtype)).ap()

    s = Sched()
    V, S, TE, GP, SY = "vector", "scalar", "tensor", "gpsimd", "sync"

    with ctx:
        I64 = sbuf("I64", 64, 64, BF16)
        negones2_t = sbuf("negones2", 66, 128, BF16)
        negones2 = [negones2_t[0:2, :], negones2_t[32:34, :], negones2_t[64:66, :]]
        ones1_64_t = sbuf("ones1_64", 33, 64, BF16)
        ones1_64 = [ones1_64_t[0:1, :], ones1_64_t[32:33, :]]
        onesrow = sbuf("onesrow", 33, 128, BF16)
        ones32_64 = sbuf("ones32_64", 32, 64, BF16)
        ones_sc32 = sbuf("ones_sc32", 32, 128, F32)
        E4hl = sbuf("E4hl", 64, NH * Q, BF16)
        E1p = sbuf("E1p", 32, DIN, BF16)
        Dbc = sbuf("Dbc", 128, DIN, BF16)
        cw_sb = sbuf("cw_sb", 128, 18 * 4, F32)
        cb_sb = sbuf("cb_sb", 128, 18, F32)
        dtb_sb = sbuf("dtb_sb", 32, 1, F32)
        cpool = sbuf("cpool", 128, 8, F32)
        A_sb = sbuf("A_sb", 32, 1, F32)
        D_sb = sbuf("D_sb", 32, 1, F32)
        normw_sb = sbuf("normw_sb", 128, 16, F32)
        rmsw_sb = sbuf("rmsw_sb", 128, 8, F32)
        b1_sb = sbuf("b1_sb", 128, 32, F32)
        bias_rows = sbuf("bias_rows", 33, D, BF16)
        b2_sb = bias_rows[0:1, :]
        fb_sb = bias_rows[32:33, :]
        wdt_sb = sbuf("wdt_sb", 128, 8 * 32, BF16)

        REG_A = sbuf("REG_A", 128, 16384, BF16)
        MEa = REG_A[:, 0:4096]
        MEb = REG_A[:, 4096:8192]
        Ctld = REG_A[0:64, 8192:12288]
        flats = [REG_A[0:2, 12288:16384], REG_A[32:34, 12288:16384],
                 REG_A[64:66, 12288:16384]]
        nqhl = None
        mid_fm = REG_A[:, 0:16384]
        wslab = REG_A[:, 0:8192].bitcast(F32)
        wslab2 = REG_A[:, 8192:12288]

        REG_B = sbuf("REG_B", 128, 4096, BF16)
        xnb_tm = REG_B[:, 0:4096]
        dxs = REG_B[:, 0:2048]
        ybuf = REG_B[:, 2048:4096]
        g_tm = REG_B[:, 0:4096]
        G_rep = REG_B[:, 0:512]
        Crep = REG_B[0:64, 512:1024]
        B_tm = sbuf("b_tm", 128, 64, BF16)

        REG_C = sbuf("REG_C", 128, 8192, BF16)
        sz = REG_C[:, 0:8192]
        yg_fm = REG_C[:, 0:8192]
        g1s = REG_C[0:4, 0:2048]
        g2s = REG_C[0:4, 2048:2560]
        k2buf = REG_C.bitcast(F32)[:, 1280:1536]

        REG_D = sbuf("REG_D", 128, 8192, BF16)
        xs_tm = REG_D[:, 0:8192]
        hnb_tm = REG_D[:, 0:4096]
        hn_fm = REG_D[:, 4096:8192]

        REG_E = sbuf("REG_E", 128, 8192, BF16)
        x_tm = REG_E.bitcast(F32)[:, 0:4096]
        yg = REG_E[:, 0:8192]
        sg_tm = REG_E.bitcast(F32)[:, 0:4096]

        REG_F = sbuf("REG_F", 128, 8192, BF16)
        xn_tm = REG_F.bitcast(F32)[:, 0:4096]
        x2_tm = REG_F.bitcast(F32)[:, 0:4096]
        out_tm = REG_F.bitcast(F32)[:, 0:4096]

        h_tm_t = sbuf("h_tm", 128, 4096, F32)
        h_tm = h_tm_t[:, 0:4096]
        dtxw = h_tm_t.bitcast(BF16)[:, 0:2048]
        decrow = h_tm_t.bitcast(BF16)[0:32, 2048:4096]
        DE = h_tm_t.bitcast(BF16)[0:32, 2048:4096]
        dts3 = h_tm_t.bitcast(F32)[0:32, 1024:1152]

        SCR = sbuf("SCR", 128, 4096, BF16)
        junkS = SCR[:, 0:2048]
        junk32 = SCR.bitcast(F32)[:, 0:1024]
        junkV = SCR[:, 2048:4096]
        stats_t = sbuf("stats_t", 128, 28, F32)
        stats = stats_t[:, 0:16]
        hss = stats_t[:, 16:20]
        rr_h = stats_t[:, 20:24]
        ss_acc = stats_t[:, 24:26]
        rstd_y = stats_t[:, 26:28] if False else None
        rstd_y2 = sbuf("rstd_y2", 128, 4, F32)
        rstd_y = rstd_y2
        st_tmp = SCR.bitcast(F32)[0:64, 0:2048]

        WBUF = sbuf("WBUF", 128, 8192, BF16)
        WB = [WBUF[:, 0:4096], WBUF[:, 4096:8192]]
        ecibc = [WBUF[0:64, i * 512:(i + 1) * 512] for i in range(4)]
        nqhl = [WBUF[0:64, 4096 + i * 128: 4096 + (i + 1) * 128] for i in range(4)]

        xn_fm = sbuf("xn_fm", 128, 8 * TTOK, BF16)
        xsbc = sbuf("xsbc", 128, 18 * TTOK, BF16)
        xbcs = sbuf("xbcs", 128, TTOK + 4, BF16)
        lds = xbcs.bitcast(F32)[0:32, 0:128]
        cums = xbcs.bitcast(F32)[0:32, 128:256]
        xhist = sbuf("xhist", 128, 18 * 3, BF16)
        PAB = sbuf("PAB", 128, 2 * TTOK, BF16)
        pAs_b = PAB[:, 0:TTOK]
        pBs_b = PAB[:, TTOK:2 * TTOK]
        dt_tile = PAB.bitcast(F32)[0:32, 0:TTOK]

        S2u = sbuf("S2u", 64, DIN, F32)
        S2bu = sbuf("S2bu", 64, DIN, BF16)
        ldt_sb = sbuf("ldt_sb", 32, 128, F32)
        dtw = sbuf("dtw_t", 32, 512, BF16)
        wv = ldt_sb.bitcast(BF16)[0:32, 0:128]
        qv = lds
        mneg = ldt_sb
        chlpack = SCR[0:32, 2048:2688]
        decay = sbuf("decay", 32, 4, F32)

        PA = ctx.enter_context(nc.psum_tensor("PA", [128, 2048], F32)).ap()
        PY = ctx.enter_context(nc.psum_tensor("PY", [128, 2048], F32)).ap()

        def PAt(i):
            return PA[:, i * 512:(i + 1) * 512]

        PK = ["PA0", "PA1", "PA2", "PA3"]

        def tap(name, ap, reads, tile):
            if DEBUG_TAPS.get(name) != tile:
                return
            o = nc.dram_tensor("dbg_" + name, list(ap.shape), ap.dtype,
                               kind="ExternalOutput").ap()
            s.op(GP, lambda e, o=o, ap=ap: e.dma_start(out=o[:], in_=ap),
                 reads=reads, dma=True)

        def dma(eng, out, in_, reads=(), writes=(), **kw):
            return s.op(eng, lambda e, out=out, in_=in_, kw=kw: e.dma_start(out=out, in_=in_, **kw),
                        reads=reads, writes=writes, dma=True)

        def mems(ap, val, key):
            s.op(V, lambda e, ap=ap, val=val: e.memset(ap, val), writes=[key])

        CONSTVALS = [EPS, 1.0, -0.5, -1.0, 1.0 / DIN, 1.0 / D]
        for ci, cv in enumerate(CONSTVALS):
            mems(cpool[:, ci:ci + 1], cv, "C:imm")
            nc.const_aps.register(F32, cv, cpool[:, ci:ci + 1])
        mems(negones2[0][:], -1.0, "C:no2")
        mems(negones2[1][:], -1.0, "C:no2")
        mems(negones2[2][:], -1.0, "C:no2")
        mems(ones1_64[0][:], 1.0, "C:o64")
        mems(ones1_64[1][:], 1.0, "C:o64")
        mems(onesrow[0:1, :], 1.0, "C:orow")
        mems(onesrow[32:33, :], 1.0, "C:orow")
        mems(ones32_64[:], 1.0, "C:o32b")
        mems(ones_sc32[:], 1.0, "C:osc")
        mems(S2u[:], 0.0, "S2u")
        mems(S2bu[:], 0.0, "S2bu")
        mems(xhist[:], 0.0, "xhist")
        mems(E4hl[:], 1.0, "C:E4")
        mems(E1p[:], 1.0, "C:E1")
        mems(I64[:], 1.0, "C:I64")
        s.op(GP, lambda e: e.affine_select(
            E4hl[0:32, :], E4hl[0:32, :], pattern=[[1, 32], [0, 128]],
            compare_op=ALU.is_equal, fill=0.0, base=0, channel_multiplier=-1),
            reads=["C:E4"], writes=["C:E4"])
        s.op(GP, lambda e: e.affine_select(
            E4hl[32:64, :], E4hl[32:64, :], pattern=[[1, 32], [0, 128]],
            compare_op=ALU.is_equal, fill=0.0, base=0, channel_multiplier=-1),
            reads=["C:E4"], writes=["C:E4"])
        s.op(GP, lambda e: e.affine_select(
            E1p[:], E1p[:], pattern=[[1, 32], [0, 64]],
            compare_op=ALU.is_equal, fill=0.0, base=0, channel_multiplier=-1),
            reads=["C:E1"], writes=["C:E1"])
        s.op(GP, lambda e: e.affine_select(
            I64[:], I64[:], pattern=[[1, 64]],
            compare_op=ALU.is_equal, fill=0.0, base=0, channel_multiplier=-1),
            reads=["C:I64"], writes=["C:I64"])

        dma(GP, cw_sb[:, 0:64].rearrange("p (fc k) -> p fc k", k=4),
            convw_h[0:2048].rearrange("(fc p) k -> p fc k", p=128), writes=["C:cw"])
        dma(GP, cw_sb[0:64, 64:68], convw_h[2048:2112, :], writes=["C:cw"])
        dma(GP, cw_sb[0:64, 68:72], convw_h[2112:2176, :], writes=["C:cw"])
        dma(GP, cb_sb[:, 0:16], convb_h[0:2048].rearrange("(fc p) -> p fc", p=128), writes=["C:cb"], allow_slow_non_contiguous=True)
        dma(GP, cb_sb[0:64, 16:17], convb_h[2048:2112].unsqueeze(1), writes=["C:cb"])
        dma(GP, cb_sb[0:64, 17:18], convb_h[2112:2176].unsqueeze(1), writes=["C:cb"])
        dma(GP, dtb_sb[:], dtb_h.unsqueeze(1), writes=["C:dtb"])
        dma(GP, D_sb[:], dpar_h.unsqueeze(1), writes=["C:D"])
        dma(GP, normw_sb[:], normw_h.rearrange("(kc p) -> p kc", p=128), writes=["C:nw"], allow_slow_non_contiguous=True)
        dma(GP, rmsw_sb[:], rmsw_h.rearrange("(kc p) -> p kc", p=128), writes=["C:rw"], allow_slow_non_contiguous=True)
        dma(GP, b1_sb[:], b1_h.rearrange("(mc p) -> p mc", p=128), writes=["C:b1"], allow_slow_non_contiguous=True)
        dma(GP, b2_sb[:], b2_h.unsqueeze(0), writes=["C:b2"])
        dma(GP, fb_sb[0:1, 0:512], fbr_h.unsqueeze(0), writes=["C:fb"])
        dma(GP, fb_sb[0:1, 512:1024], fbi_h.unsqueeze(0), writes=["C:fb"])
        dma(GP, A_sb[:], alog_h.unsqueeze(1), writes=["C:A"])
        s.op(S, lambda e: e.activation(out=A_sb[:], in_=A_sb[:], func=AF.Exp),
             reads=["C:A"], writes=["C:A"])
        s.op(V, lambda e: e.tensor_scalar_mul(out=A_sb[:], in0=A_sb[:], scalar1=-1.0),
             reads=["C:A"], writes=["C:A"])
        dma(GP, wdt_sb.rearrange("p (kc m) -> p kc m", m=32),
            w_in_h[:, 4224:4256].rearrange("(kc p) m -> p kc m", p=128), writes=["C:wdt"])

        s.op(V, lambda e: e.tensor_scalar_mul(out=DE[:], in0=E1p[:], scalar1=D_sb[:]),
             reads=["C:E1", "C:D"], writes=["h"])
        for nn in range(4):
            s.op(TE, lambda e, nn=nn: e.matmul(
                PY[0:64, nn * 512:(nn + 1) * 512], ones32_64[:],
                DE[:, nn * 512:(nn + 1) * 512], start=True, stop=True),
                reads=["h", "C:o32b"], writes=["PYh0", "PYh1"])
            s.op(TE, lambda e, nn=nn: e.matmul(
                PY[64:128, nn * 512:(nn + 1) * 512], ones32_64[:],
                DE[:, nn * 512:(nn + 1) * 512], start=True, stop=True,
                tile_position=(0, 64)),
                reads=["h", "C:o32b"], writes=["PYh0", "PYh1"])
        s.op(V, lambda e: e.tensor_copy(out=Dbc[:], in_=PY[:, 0:DIN]),
             reads=["PYh0", "PYh1"], writes=["C:Dbc"])

        dma(GP, g1s[:, 0:1024].rearrange("c (b d) -> c b d", d=32),
            g1r_h.rearrange("b c d -> c b d"), writes=["C", "C:g1"])
        dma(GP, g1s[:, 1024:2048].rearrange("c (b d) -> c b d", d=32),
            g1i_h.rearrange("b c d -> c b d"), writes=["C", "C:g1"])
        dma(GP, g2s[:, 0:256], g2r_h.rearrange("c e g -> c (e g)"), writes=["C", "C:g2"])
        dma(GP, g2s[:, 256:512], g2i_h.rearrange("c e g -> c (e g)"), writes=["C", "C:g2"])
        bfR = [REG_C[0:32, 3072:3584], REG_C[0:32, 4608:5120]]
        bfI = [REG_C[0:32, 3584:4096], REG_C[0:32, 5120:5632]]
        bfnI = [REG_C[0:32, 4096:4608], REG_C[0:32, 5632:6144]]
        wcv = wc_sc.rearrange("(dd8 g2) (kc ch c) -> g2 kc dd8 ch c", g2=16, ch=2, c=512)

        def emit_wc_rearrange():
            for d8 in range(4):
                for chh in range(2):
                    for half in range(2):
                        blk = {(0, 0): 0, (0, 1): 1, (1, 0): 2, (1, 1): 3}[(half, chh)]
                        dma(GP, wcv[:, half * 4 + d8, :, chh, :].rearrange("g2 dd8 c -> dd8 g2 c"),
                            wc2_sc[:, blk, d8 * 8:(d8 + 1) * 8, :].rearrange("g2 dd8 c -> dd8 g2 c"),
                            reads=["D:wc2"], writes=["D:wc"])
        for g in range(16):
            for ri in range(2):
                pi = ri
                for b in range(32):
                    s.op(TE, lambda e, g=g, ri=ri, b=b, pi=pi: e.matmul(
                        PAt(pi)[0:32, b * 16:(b + 1) * 16],
                        g1s[:, ri * 1024 + b * 32: ri * 1024 + (b + 1) * 32],
                        g2s.rearrange("c (e g2) -> c g2 e", g2=16)[:, g + ri * 16 if False else g, :] if False else g2s[:, ri * 256:(ri + 1) * 256].rearrange("c (e g2) -> c g2 e", g2=16)[:, g, :],
                        start=True, stop=True),
                        reads=["C:g1", "C:g2"], writes=[PK[pi]])
            gb = g % 2
            s.op(V, lambda e, gb=gb: e.tensor_copy(out=bfR[gb][:], in_=PAt(0)[0:32, :]),
                 reads=["PA0"], writes=["C", f"k2buf{gb}"])
            s.op(V, lambda e, gb=gb: e.tensor_copy(out=bfI[gb][:], in_=PAt(1)[0:32, :]),
                 reads=["PA1"], writes=["C", f"k2buf{gb}"])
            s.op(V, lambda e, gb=gb: e.tensor_scalar_mul(out=bfnI[gb][:], in0=PAt(1)[0:32, :], scalar1=-1.0),
                 reads=["PA1"], writes=["C", f"k2buf{gb}"])
            dma(GP, wc2_sc[g, 0], bfR[gb][:], reads=[f"k2buf{gb}"], writes=["D:wc2"])
            dma(GP, wc2_sc[g, 1], bfI[gb][:], reads=[f"k2buf{gb}"], writes=["D:wc2"])
            dma(GP, wc2_sc[g, 2], bfnI[gb][:], reads=[f"k2buf{gb}"], writes=["D:wc2"])
            dma(GP, wc2_sc[g, 3], bfR[gb][:], reads=[f"k2buf{gb}"], writes=["D:wc2"])

        tap("E4hl", E4hl[:], ["C:E4"], 0)
        tap("E1p", E1p[:], ["C:E1"], 0)
        tap("I64", I64[:], ["C:I64"], 0)

        emit_wc_rearrange()

        for nz in range(4):
            dma(GP, wz_sc[nz].rearrange("kp (kc n) -> kp kc n", n=512),
                w_in_h[:, nz * 512:(nz + 1) * 512].rearrange("(kc kp) n -> kp kc n", kp=128),
                writes=["D:wz"])
        for mc in range(17):
            dma(GP, wxbc_sc[mc],
                w_in_h[:, 2048 + mc * 128: 2048 + (mc + 1) * 128].rearrange("(kc kp) m -> kp kc m", kp=128),
                writes=["D:wxbc"])
        for sess in range(4):
            for nd in range(2):
                dma(GP, w2_sc[nd * 4 + sess].rearrange("kp (kc n) -> kp kc n", n=512),
                    w2_h[sess * 1024:(sess + 1) * 1024, nd * 512:(nd + 1) * 512].rearrange("(kc kp) n -> kp kc n", kp=128),
                    writes=["D:w2"])
        def emit_wout_fold():
            for kc in range(16):
                dma(GP, wslab[:, 0:D], wout_h.rearrange("(kc kp) d -> kc kp d", kp=128)[kc],
                    writes=["A"])
                s.op(V, lambda e, kc=kc: e.tensor_scalar_mul(
                    out=wslab2[:, 0:D], in0=wslab[:, 0:D], scalar1=normw_sb[:, kc:kc + 1]),
                    reads=["A", "C:nw"], writes=["A"])
                sess = kc // 8
                for nd in range(2):
                    dma(GP, wout_sc[nd * 2 + sess, :, (kc % 8) * 512:(kc % 8 + 1) * 512],
                        wslab2[:, nd * 512:(nd + 1) * 512], reads=["A"], writes=["D:wout"])

        def emit_w1_fold():
            for kc in range(8):
                dma(GP, wslab[:], w1_h.rearrange("(kc kp) m -> kc kp m", kp=128)[kc],
                    writes=["A"])
                s.op(V, lambda e, kc=kc: e.tensor_scalar_mul(
                    out=wslab2[:], in0=wslab[:], scalar1=rmsw_sb[:, kc:kc + 1]),
                    reads=["A", "C:rw"], writes=["A"])
                dma(GP, w1_sc[:, :, kc, :].rearrange("mc kp m -> kp mc m"),
                    wslab2.rearrange("kp (mc m) -> kp mc m", m=128),
                    reads=["A"], writes=["D:w1"])

        for tile in range(n_tiles):
            t0 = tile * TTOK

            dma(SY, x_tm.rearrange("p (g d) -> p g d", d=D),
                x_h[t0:t0 + TTOK, :].rearrange("(g p) d -> p g d", p=128),
                writes=["E"])

            tap("x_in", x_tm[:], ["E"], tile)
            s.op(V, lambda e: e.memset(stats[:], 0.0), writes=["stats"])
            for g in range(4):
                s.op(V, lambda e, g=g: e.tensor_scalar(
                    out=junk32[:], in0=x_tm[:, g * D:(g + 1) * D], scalar1=1.0 / D,
                    scalar2=None, op0=ALU.mult, op1=ALU.add,
                    accum_out=stats[:, g:g + 1]),
                    reads=["E"], writes=["SCR", "stats"])
            for g in range(4):
                s.op(S, lambda e, g=g: e.activation(
                    out=junk32[:], in_=x_tm[:, g * D:(g + 1) * D], func=AF.Square,
                    accum_out=stats[:, 4 + g:5 + g]),
                    reads=["E"], writes=["SCR", "stats"])
            s.op(V, lambda e: e.tensor_mul(out=stats[:, 8:12], in0=stats[:, 0:4],
                                           in1=stats[:, 0:4]),
                 reads=["stats"], writes=["stats"])
            s.op(V, lambda e: e.scalar_tensor_tensor(
                out=stats[:, 8:12], in0=stats[:, 4:8], scalar=1.0 / D,
                in1=stats[:, 8:12], op0=ALU.mult, op1=ALU.subtract),
                reads=["stats", "C:imm"], writes=["stats"])
            s.op(S, lambda e: e.activation(out=stats[:, 8:12], in_=stats[:, 8:12],
                                           func=AF.Ln, bias=EPS),
                 reads=["stats", "C:imm"], writes=["stats"])
            s.op(S, lambda e: e.activation(out=stats[:, 12:16], in_=stats[:, 8:12],
                                           func=AF.Exp, scale=-0.5),
                 reads=["stats"], writes=["stats"])
            s.op(V, lambda e: e.scalar_tensor_tensor(
                out=stats[:, 0:4], in0=stats[:, 0:4], scalar=-1.0, in1=stats[:, 12:16],
                op0=ALU.mult, op1=ALU.mult),
                reads=["stats"], writes=["stats"])
            tap("stats", stats[:], ["stats"], tile)
            for g in range(4):
                s.op(S, lambda e, g=g: e.activation(
                    out=xnb_tm[:, g * D:(g + 1) * D], in_=x_tm[:, g * D:(g + 1) * D],
                    func=AF.Identity, bias=stats[:, g:g + 1], scale=stats[:, 12 + g:13 + g]),
                    reads=["E", "stats"], writes=["B"])
                s.op(V, lambda e, g=g: e.tensor_scalar(
                    out=xn_tm[:, g * D:(g + 1) * D], in0=x_tm[:, g * D:(g + 1) * D],
                    scalar1=stats[:, 12 + g:13 + g], scalar2=stats[:, g:g + 1],
                    op0=ALU.mult, op1=ALU.add),
                    reads=["E", "stats"], writes=["F"])
            for g in range(4):
                s.op(SY, lambda e, g=g: e.dma_start_transpose(
                    out=xn_fm.rearrange("p (kc t) -> p kc t", t=TTOK)[:, :, g * 128:(g + 1) * 128],
                    in_=xnb_tm[:, g * D:(g + 1) * D]),
                    reads=["B"], writes=["xnfm"], dma=True)
            tap("xn_tm", xn_tm[:], ["F"], tile)
            tap("xn_fm", xn_fm[:], ["xnfm"], tile)

            def emit_prep(c):
                b = c
                fb = c % 3
                cs = slice(c * Q, (c + 1) * Q)
                dtv = dts3[:, 0:128] if c == 3 else dt_tile[:, cs]
                dtk = "dt3" if c == 3 else "dt"
                s.op(V, lambda e, dtv=dtv: e.tensor_scalar_mul(
                    out=lds[:], in0=dtv, scalar1=A_sb[:]),
                    reads=[dtk, "C:A", "xbcs0"] + (["h"] if c == 3 else []), writes=["lds", "xbcs0"])
                s.op(V, lambda e: e.tensor_tensor_scan(
                    out=cums[:], data0=ones_sc32[:], data1=lds[:], initial=0.0,
                    op0=ALU.mult, op1=ALU.add),
                    reads=["lds", "C:osc"], writes=["cums", "xbcs0"])
                s.op(S, lambda e, dtv=dtv: e.activation(
                    out=ldt_sb[:], in_=dtv, func=AF.Ln),
                    reads=[dtk] + (["h"] if c == 3 else []), writes=["ldt"])
                s.op(V, lambda e: e.tensor_sub(out=qv[:], in0=cums[:], in1=ldt_sb[:]),
                     reads=["cums", "ldt"], writes=["lds", "xbcs0"])
                s.op(V, lambda e: e.tensor_scalar_mul(
                    out=chlpack[:, 0:128], in0=cums[:], scalar1=-1.0),
                    reads=["cums"], writes=["chp", "pAs", "SCR"])
                s.op(V, lambda e: e.tensor_scalar_mul(
                    out=mneg[:], in0=cums[:], scalar1=-1.0),
                    reads=["cums"], writes=["ldt"])
                s.op(V, lambda e: e.tensor_sub(
                    out=chlpack[:, 128:256], in0=mneg[:], in1=chlpack[:, 0:128]),
                    reads=["ldt", "chp"], writes=["chp", "pAs", "SCR"])
                s.op(V, lambda e: e.tensor_scalar_mul(
                    out=chlpack[:, 256:384], in0=qv[:], scalar1=-1.0),
                    reads=["lds"], writes=["chp", "pAs", "SCR"])
                s.op(V, lambda e: e.tensor_scalar_mul(
                    out=mneg[:], in0=qv[:], scalar1=-1.0),
                    reads=["lds"], writes=["ldt"])
                s.op(V, lambda e: e.tensor_sub(
                    out=chlpack[:, 384:512], in0=mneg[:], in1=chlpack[:, 256:384]),
                    reads=["ldt", "chp"], writes=["chp", "pAs", "SCR"])
                s.op(S, lambda e: e.activation(
                    out=chlpack[:, 512:640], in_=cums[:], func=AF.Exp),
                    reads=["cums"], writes=["chp", "pAs", "SCR"])
                s.op(S, lambda e: e.activation(
                    out=wv[:], in_=cums[:], func=AF.Exp, scale=-1.0, bias=cums[:, 127:128]),
                    reads=["cums", "ldt", "C:imm"], writes=["ldt"])
                s.op(S, lambda e, c=c: e.activation(
                    out=decay[:, c:c + 1], in_=cums[:, 127:128], func=AF.Exp),
                    reads=["cums"], writes=[f"dcy{c}"])
                s.op(V, lambda e, dtv=dtv, c=c: e.tensor_mul(
                    out=dtw[:, c * 128:(c + 1) * 128], in0=dtv, in1=wv[:]),
                    reads=[dtk, "ldt", "cums"] + (["h"] if c == 3 else []), writes=[f"dtw{c}"])
                dma(SY, chl_sc[b].rearrange("r (h i) -> h r i", i=128),
                    chlpack.rearrange("h (r i) -> h r i", i=128),
                    reads=["chp", "pAs", "SCR"], writes=[f"D:chl{b}"])
                dma(SY, flats[fb][:, 0:4096], chl_sc[b, 0:2, :],
                    reads=[f"D:chl{b}"], writes=["A", f"flat{fb}"])

            for kc in range(8):
                s.op(TE, lambda e, kc=kc: e.matmul(
                    PAt(3)[0:32, :], wdt_sb[:, kc * 32:(kc + 1) * 32],
                    xn_fm[:, kc * TTOK:(kc + 1) * TTOK],
                    start=(kc == 0), stop=(kc == 7)),
                    reads=["C:wdt", "xnfm"], writes=["PA3"])
            s.op(S, lambda e: e.activation(out=dt_tile[:], in_=PAt(3)[0:32, :],
                                           func=AF.Exp, bias=dtb_sb[:]),
                 reads=["PA3", "C:dtb", "pAs", "pBs"], writes=["dt", "pAs", "pBs"])
            s.op(S, lambda e: e.activation(out=dt_tile[:], in_=dt_tile[:],
                                           func=AF.Ln, bias=1.0),
                 reads=["dt", "C:imm"], writes=["dt", "pAs", "pBs"])
            s.op(V, lambda e: e.tensor_scalar_max(out=dt_tile[:], in0=dt_tile[:],
                                                  scalar1=1e-30),
                 reads=["dt"], writes=["dt", "pAs", "pBs"])
            tap("dt", dt_tile[:], ["dt"], tile)
            s.op(V, lambda e: e.tensor_copy(out=dts3[:], in_=dt_tile[:, 3 * Q:4 * Q]),
                 reads=["dt", "h"], writes=["dt3", "h"])

            for cpp in range(3):
                emit_prep(cpp)

            for nz in range(4):
                hb = nz % 2
                dma(SY, WB[hb], wz_sc[nz], reads=["D:wz"], writes=[f"WB{hb}"])
                for ig in range(4):
                    pi = (nz * 4 + ig) % 4
                    for kc in range(8):
                        s.op(TE, lambda e, ig=ig, kc=kc, hb=hb, pi=pi: e.matmul(
                            PAt(pi),
                            xn_fm[:, kc * TTOK + ig * 128: kc * TTOK + (ig + 1) * 128],
                            WB[hb][:, kc * 512:(kc + 1) * 512],
                            start=(kc == 0), stop=(kc == 7)),
                            reads=["xnfm", f"WB{hb}"], writes=[PK[pi]])
                    s.op(S, lambda e, ig=ig, nz=nz, pi=pi: e.activation(
                        out=sz[:, ig * DIN + nz * 512: ig * DIN + (nz + 1) * 512],
                        in_=PAt(pi), func=AF.Silu),
                        reads=[PK[pi]], writes=["C", f"sz{ig}"])
            tap("sz", sz[:], ["C"], tile)

            for mc in range(17):
                hb = (mc // 4) % 2
                if mc % 4 == 0:
                    nt_ = min(4, 17 - mc)
                    dma(SY, WB[hb][:, 0:nt_ * 1024].rearrange("p (q km) -> p q km", km=1024),
                        wxbc_sc[mc:mc + nt_].rearrange("q kp kc m -> kp q (kc m)"),
                        reads=["D:wxbc"], writes=[f"WB{hb}"])
                base = (mc % 4) * 1024
                subs = [(slice(0, 128), mc)] if mc < 16 else [(slice(0, 64), 16), (slice(0, 64), 17)]
                for si, (pr, fc) in enumerate(subs):
                    pi = (mc + si) % 4
                    for kc in range(8):
                        moff = base + (kc * 128 if mc < 16 else kc * 128 + si * 64)
                        mw = 128 if mc < 16 else 64
                        s.op(TE, lambda e, kc=kc, hb=hb, pi=pi, moff=moff, mw=mw, pr=pr: e.matmul(
                            PAt(pi)[pr, :],
                            WB[hb][:, moff: moff + mw],
                            xn_fm[:, kc * TTOK:(kc + 1) * TTOK],
                            start=(kc == 0), stop=(kc == 7)),
                            reads=[f"WB{hb}", "xnfm"], writes=[PK[pi]])
                    bb = 0
                    xb = xbcs[:, 0:TTOK + 3]
                    s.op(V, lambda e, xb=xb, pr=pr, pi=pi: e.tensor_copy(
                        out=xb[pr, 3:TTOK + 3], in_=PAt(pi)[pr, :]),
                        reads=[PK[pi]], writes=[f"xbcs{bb}"])
                    s.op(V, lambda e, xb=xb, pr=pr, fc=fc: e.tensor_copy(
                        out=xb[pr, 0:3], in_=xhist[pr, fc * 3:(fc + 1) * 3]),
                        reads=["xhist"], writes=[f"xbcs{bb}"])
                    s.op(V, lambda e, xb=xb, pr=pr, fc=fc: e.tensor_copy(
                        out=xhist[pr, fc * 3:(fc + 1) * 3], in_=xb[pr, TTOK:TTOK + 3]),
                        reads=[f"xbcs{bb}"], writes=["xhist"])
                    s.op(V, lambda e, xb=xb, pr=pr, fc=fc: e.tensor_scalar_mul(
                        out=pAs_b[pr, :], in0=xb[pr, 0:TTOK], scalar1=cw_sb[pr, fc * 4:fc * 4 + 1]),
                        reads=[f"xbcs{bb}", "C:cw"], writes=["pAs"])
                    s.op(V, lambda e, xb=xb, pr=pr, fc=fc: e.scalar_tensor_tensor(
                        out=pAs_b[pr, :], in0=xb[pr, 1:1 + TTOK], scalar=cw_sb[pr, fc * 4 + 1:fc * 4 + 2],
                        in1=pAs_b[pr, :], op0=ALU.mult, op1=ALU.add),
                        reads=[f"xbcs{bb}", "pAs"], writes=["pAs"])
                    s.op(V, lambda e, xb=xb, pr=pr, fc=fc: e.tensor_scalar_mul(
                        out=pBs_b[pr, :], in0=xb[pr, 2:2 + TTOK], scalar1=cw_sb[pr, fc * 4 + 2:fc * 4 + 3]),
                        reads=[f"xbcs{bb}", "C:cw"], writes=["pBs"])
                    s.op(V, lambda e, xb=xb, pr=pr, fc=fc: e.scalar_tensor_tensor(
                        out=pBs_b[pr, :], in0=xb[pr, 3:3 + TTOK], scalar=cw_sb[pr, fc * 4 + 3:fc * 4 + 4],
                        in1=pBs_b[pr, :], op0=ALU.mult, op1=ALU.add),
                        reads=[f"xbcs{bb}", "pBs"], writes=["pBs"])
                    s.op(V, lambda e, pr=pr: e.tensor_add(
                        out=pAs_b[pr, :], in0=pAs_b[pr, :], in1=pBs_b[pr, :]),
                        reads=["pAs", "pBs"], writes=["pAs"])
                    s.op(S, lambda e, pr=pr, fc=fc: e.activation(
                        out=xsbc[pr, fc * TTOK:(fc + 1) * TTOK], in_=pAs_b[pr, :],
                        func=AF.Silu, bias=cb_sb[pr, fc:fc + 1]),
                        reads=["pAs", "C:cb"], writes=[f"xsbc{fc}"])
            tap("xsbc", xsbc[:], [f"xsbc{i}" for i in range(18)], tile)

            for fc in range(16):
                s.op(SY, lambda e, fc=fc: e.dma_start_transpose(
                    out=xs_tm.rearrange("p (c f) -> p c f", f=DIN)[:, :, fc * 128:(fc + 1) * 128],
                    in_=xsbc[:, fc * TTOK:(fc + 1) * TTOK]),
                    reads=[f"xsbc{fc}"], writes=["D", "xstm0", "xstm1", "xstm2", "xstm3"], dma=True)

            if tile == 0:
                emit_wout_fold()
                emit_w1_fold()
            def load_nqhl(cc):
                nb2 = cc % 2
                dma(SY, nqhl[nb2][0:32, :], chl_sc[cc, 2, :].rearrange("(h i) -> h i", i=128),
                    reads=[f"D:chl{cc}"], writes=["WB1", f"nq{nb2}"])
                dma(SY, nqhl[nb2][32:64, :], chl_sc[cc, 3, :].rearrange("(h i) -> h i", i=128),
                    reads=[f"D:chl{cc}"], writes=["WB1", f"nq{nb2}"])
            load_nqhl(0)
            for c in range(NCH):
                b = c
                cs = slice(c * Q, (c + 1) * Q)
                B_fm = xsbc[0:64, 16 * TTOK + c * 128: 16 * TTOK + (c + 1) * 128]
                C_fm = xsbc[0:64, 17 * TTOK + c * 128: 17 * TTOK + (c + 1) * 128]
                xs_c = xs_tm[:, c * DIN:(c + 1) * DIN]

                nb = c % 2
                s.op(TE, lambda e, B_fm=B_fm, C_fm=C_fm: e.matmul(
                    PAt(3)[:, 0:128], B_fm, C_fm, start=True, stop=True),
                    reads=["xsbc16", "xsbc17"], writes=["PA3"])
                s.op(V, lambda e: e.tensor_copy(out=G_rep[:, 0:128], in_=PAt(3)[:, 0:128]),
                     reads=["PA3"], writes=["B", "Grep"])
                for r in range(1, 4):
                    s.op(V, lambda e, r=r: e.tensor_copy(
                        out=G_rep[:, r * 128:(r + 1) * 128], in_=G_rep[:, 0:128]),
                        reads=["Grep"], writes=["B", "Grep"])
                for r in range(4):
                    s.op(V, lambda e, r=r, C_fm=C_fm: e.tensor_copy(
                        out=Crep[:, r * 128:(r + 1) * 128], in_=C_fm),
                        reads=["xsbc17"], writes=["B", "Crep"])
                s.op(TE, lambda e, B_fm=B_fm: e.matmul(
                    PAt(3)[:, 0:64], B_fm, I64[:], start=True, stop=True),
                    reads=["xsbc16", "C:I64"], writes=["PA3"])
                s.op(V, lambda e: e.tensor_copy(out=B_tm[:], in_=PAt(3)[:, 0:64]),
                     reads=["PA3"], writes=["B", "Btm"])

                for nn in range(4):
                    s.op(TE, lambda e, nn=nn, c=c: e.matmul(
                        PY[:, nn * 512:(nn + 1) * 512], dtw[:, c * 128:(c + 1) * 128],
                        E1p[:, nn * 512:(nn + 1) * 512], start=True, stop=True),
                        reads=[f"dtw{c}", "C:E1", "PYh0" if nn < 2 else "PYh1"],
                        writes=["PYh0" if nn < 2 else "PYh1"])
                s.op(V, lambda e, xs_c=xs_c, c=c: e.tensor_mul(
                    out=dtxw[:], in0=xs_c, in1=PY[:, 0:DIN]),
                    reads=[f"xstm{c}", "PYh0", "PYh1"], writes=["h"])
                if c > 0:
                    s.op(V, lambda e, c=c: e.tensor_scalar_mul(
                        out=decrow[:], in0=E1p[:], scalar1=decay[:, c:c + 1]),
                        reads=["C:E1", f"dcy{c}"], writes=["h"])
                fb = c % 3

                def emit_ymm1(g, xs_c=xs_c, c=c):
                    for hh in range(g * 4, g * 4 + 4):
                        s.op(TE, lambda e, hh=hh, xs_c=xs_c: e.matmul(
                            PY[:, hh * HD:(hh + 1) * HD],
                            MEb[:, hh * 128:(hh + 1) * 128], xs_c[:, hh * HD:(hh + 1) * HD],
                            start=(hh % 8 == 0), stop=False),
                            reads=[f"MEb{hh // 4}", f"xstm{c}"],
                            writes=["PYh0" if hh < 16 else "PYh1", "PYy"])

                for g8 in range(8):
                    gs = slice(g8 * 512, (g8 + 1) * 512)
                    pi = g8 % 4
                    cj = g8 % 4
                    dma(SY, ecibc[cj][:],
                        chl_sc[b, 4, g8 * 512:(g8 + 1) * 512].unsqueeze(0).broadcast_to([64, 512]),
                        reads=[f"D:chl{b}"], writes=["WB0", f"ecb{cj}"])
                    s.op(TE, lambda e, gs=gs, pi=pi, fb=fb: e.matmul(
                        PAt(pi), negones2[fb][:], flats[fb][:, gs], start=True, stop=False),
                        reads=[f"flat{fb}", "C:no2"], writes=[PK[pi]])
                    s.op(TE, lambda e, gs=gs, pi=pi, nb=nb: e.matmul(
                        PAt(pi), nqhl[nb][:], E4hl[:, gs], start=False, stop=True),
                        reads=[f"nq{nb}", "C:E4"], writes=[PK[pi]])
                    s.op(S, lambda e, gs=gs, pi=pi: e.activation(
                        out=MEa[:, gs], in_=PAt(pi), func=AF.Exp),
                        reads=[PK[pi]], writes=["A", f"MEa{g8}"])
                    s.op(GP, lambda e, gs=gs: e.affine_select(
                        MEb[:, gs], MEa[:, gs], pattern=[[0, 4], [1, 128]],
                        compare_op=ALU.is_ge, fill=0.0, base=0, channel_multiplier=-1),
                        reads=[f"MEa{g8}"], writes=["A", f"MEb{g8}"])
                    s.op(V, lambda e, gs=gs: e.tensor_mul(
                        out=MEb[:, gs], in0=MEb[:, gs], in1=G_rep[:]),
                        reads=[f"MEb{g8}", "Grep"], writes=["A", f"MEb{g8}"])
                    s.op(V, lambda e, gs=gs, cj=cj: e.tensor_mul(
                        out=Ctld[:, gs], in0=Crep[:], in1=ecibc[cj][:]),
                        reads=["Crep", f"ecb{cj}"], writes=["A", f"Ctld{g8}"])
                    if g8 >= 2:
                        emit_ymm1(g8 - 2)

                if c + 3 < NCH:
                    emit_prep(c + 3)
                if c + 1 < NCH:
                    load_nqhl(c + 1)
                emit_ymm1(6)
                emit_ymm1(7)
                for hh in range(NH):
                    hs = slice(hh * HD, (hh + 1) * HD)
                    s.op(TE, lambda e, hh=hh, hs=hs: e.matmul(
                        PY[:, hs], Ctld[:, hh * 128:(hh + 1) * 128],
                        S2bu[:, hh * HD:(hh + 1) * HD], start=False, stop=True),
                        reads=[f"Ctld{hh // 4}", "S2bu", "PYy"],
                        writes=["PYh0" if hh < 16 else "PYh1"])
                tap(f"MEb_c{c}", MEb[:], [f"MEb{g} " .strip() for g in range(8)], tile)
                tap(f"flats_c{c}", flats[fb][:], [f"flat{fb}"], tile)
                tap(f"nqhl_c{c}", nqhl[nb][:], [f"nq{nb}"], tile)
                s.op(V, lambda e, xs_c=xs_c, c=c: e.tensor_mul(
                    out=dxs[:], in0=xs_c, in1=Dbc[:]),
                    reads=[f"xstm{c}", "C:Dbc"], writes=["B"])
                for hf in range(2):
                    s.op(V, lambda e, hf=hf: e.scalar_tensor_tensor(
                        out=ybuf[:, hf * 1024:(hf + 1) * 1024],
                        in0=PY[:, hf * 1024:(hf + 1) * 1024], scalar=1.0,
                        in1=dxs[:, hf * 1024:(hf + 1) * 1024],
                        op0=ALU.mult, op1=ALU.add),
                        reads=[f"PYh{hf}", "B"], writes=["B"])
                tap(f"ssd_y{c}", ybuf[:], ["B"], tile)
                s.op(V, lambda e, c=c: e.tensor_mul(
                    out=yg[:, c * DIN:(c + 1) * DIN], in0=ybuf[:],
                    in1=sz[:, c * DIN:(c + 1) * DIN]),
                    reads=["B", "C", f"sz{c}"], writes=["E", f"yg{c}"])
                for hf in range(2):
                    s.op(S, lambda e, c=c, hf=hf: e.activation(
                        out=junk32[:], in_=yg[:, c * DIN + hf * 1024: c * DIN + (hf + 1) * 1024],
                        func=AF.Square, accum_out=ss_acc[:, hf:hf + 1]),
                        reads=[f"yg{c}", "SCR"], writes=["SCR", "ssacc"])
                s.op(V, lambda e: e.tensor_add(
                    out=ss_acc[:, 0:1], in0=ss_acc[:, 0:1], in1=ss_acc[:, 1:2]),
                    reads=["ssacc"], writes=["ssacc"])
                s.op(S, lambda e: e.activation(
                    out=ss_acc[:, 0:1], in_=ss_acc[:, 0:1], func=AF.Ln, scale=1.0 / DIN, bias=EPS),
                    reads=["ssacc", "C:imm"], writes=["ssacc"])
                s.op(S, lambda e, c=c: e.activation(
                    out=rstd_y[:, c:c + 1], in_=ss_acc[:, 0:1], func=AF.Exp, scale=-0.5),
                    reads=["ssacc", "C:imm"], writes=["rstdy"])
                for nn in range(4):
                    s.op(TE, lambda e, nn=nn: e.matmul(
                        PY[0:64, nn * 512:(nn + 1) * 512], B_tm[:],
                        dtxw[:, nn * 512:(nn + 1) * 512], start=True, stop=True),
                        reads=["Btm", "h", "PYh0" if nn < 2 else "PYh1"],
                        writes=["PYh0" if nn < 2 else "PYh1"])
                if c == 0:
                    s.op(V, lambda e, c=c: e.tensor_scalar_mul(
                        out=decrow[:], in0=E1p[:], scalar1=decay[:, c:c + 1]),
                        reads=["C:E1", f"dcy{c}"], writes=["h"])
                for nn in range(4):
                    s.op(TE, lambda e, nn=nn: e.matmul(
                        PA[0:64, nn * 512:(nn + 1) * 512], ones32_64[:],
                        decrow[:, nn * 512:(nn + 1) * 512], start=True, stop=True),
                        reads=["h", "C:o32b"], writes=[PK[nn]])
                s.op(V, lambda e: e.tensor_mul(
                    out=st_tmp[:], in0=S2u[:], in1=PA[0:64, :]),
                    reads=["S2u", "PA0", "PA1", "PA2", "PA3"], writes=["SCR"])
                s.op(V, lambda e: e.scalar_tensor_tensor(
                    out=S2u[:], in0=PY[0:64, :], scalar=1.0, in1=st_tmp[:],
                    op0=ALU.mult, op1=ALU.add),
                    reads=["PYh0", "PYh1", "SCR"], writes=["S2u"])
                s.op(V, lambda e: e.tensor_copy(out=S2bu[:], in_=S2u[:]),
                     reads=["S2u"], writes=["S2bu"])

            tap("yg", yg[:], ["E"], tile)
            tap("S2u", S2u[:], ["S2u"], tile)

            for c in range(NCH):
                s.op(SY, lambda e, c=c: e.dma_start_transpose(
                    out=yg_fm.rearrange("p (fc t) -> p fc t", t=TTOK)[:, :, c * 128:(c + 1) * 128],
                    in_=yg[:, c * DIN:(c + 1) * DIN]),
                    reads=[f"yg{c}"], writes=["C"], dma=True)
            tap("yg_fm", yg_fm[:], ["C"], tile)
            tap("rstdy", rstd_y[:], ["rstdy"], tile)
            for nd in range(2):
                for ig in range(4):
                    pi = ig
                    for sess in range(2):
                        hb = sess
                        if ig == 0:
                            dma(SY, WB[hb], wout_sc[nd * 2 + sess],
                                reads=["D:wout"], writes=[f"WB{hb}"])
                        for kc in range(8):
                            kcg = sess * 8 + kc
                            s.op(TE, lambda e, ig=ig, kc=kc, hb=hb, pi=pi, kcg=kcg: e.matmul(
                                PAt(pi),
                                yg_fm[:, kcg * TTOK + ig * 128: kcg * TTOK + (ig + 1) * 128],
                                WB[hb][:, kc * 512:(kc + 1) * 512],
                                start=(kcg == 0), stop=(kcg == 15)),
                                reads=["C", f"WB{hb}"], writes=[PK[pi]])
                    s.op(V, lambda e, ig=ig, nd=nd, pi=pi: e.scalar_tensor_tensor(
                        out=h_tm[:, ig * D + nd * 512: ig * D + (nd + 1) * 512],
                        in0=PAt(pi), scalar=rstd_y[:, ig:ig + 1],
                        in1=xn_tm[:, ig * D + nd * 512: ig * D + (nd + 1) * 512],
                        op0=ALU.mult, op1=ALU.add),
                        reads=[PK[pi], "rstdy", "F"], writes=["h"])
            tap("h_tm", h_tm[:], ["h"], tile)

            s.op(V, lambda e: e.memset(hss[:], 0.0), writes=["hss"])
            for g in range(4):
                s.op(S, lambda e, g=g: e.activation(
                    out=junk32[:], in_=h_tm[:, g * D:(g + 1) * D], func=AF.Square,
                    accum_out=hss[:, g:g + 1]),
                    reads=["h", "hss"], writes=["SCR", "hss"])
            s.op(S, lambda e: e.activation(out=hss[:], in_=hss[:], func=AF.Ln,
                                           scale=1.0 / D, bias=EPS),
                 reads=["hss", "C:imm"], writes=["hss"])
            s.op(S, lambda e: e.activation(out=rr_h[:], in_=hss[:], func=AF.Exp,
                                           scale=-0.5),
                 reads=["hss", "C:imm"], writes=["rrh"])
            for g in range(4):
                s.op(V, lambda e, g=g: e.tensor_scalar_mul(
                    out=hnb_tm[:, g * D:(g + 1) * D], in0=h_tm[:, g * D:(g + 1) * D],
                    scalar1=rr_h[:, g:g + 1]),
                    reads=["h", "rrh"], writes=["D", "hnb"])
            for g in range(4):
                s.op(SY, lambda e, g=g: e.dma_start_transpose(
                    out=hn_fm.rearrange("p (kc t) -> p kc t", t=TTOK)[:, :, g * 128:(g + 1) * 128],
                    in_=hnb_tm[:, g * D:(g + 1) * D]),
                    reads=["hnb"], writes=["D", "hnfm"], dma=True)

            for mg in range(8):
                hb = mg % 2
                dma(SY, WB[hb][:, 0:4096].rearrange("p (q km) -> p q km", q=4),
                    w1_sc[mg * 4:(mg + 1) * 4].rearrange("q kp kc m -> kp q (kc m)"),
                    reads=["D:w1"], writes=[f"WB{hb}"])
                for mq in range(4):
                    mc = mg * 4 + mq
                    pi = mc % 4
                    for kc in range(8):
                        s.op(TE, lambda e, mq=mq, kc=kc, hb=hb, pi=pi: e.matmul(
                            PAt(pi),
                            WB[hb][:, mq * 1024 + kc * 128: mq * 1024 + (kc + 1) * 128],
                            hn_fm[:, kc * TTOK:(kc + 1) * TTOK],
                            start=(kc == 0), stop=(kc == 7)),
                            reads=[f"WB{hb}", "hnfm"], writes=[PK[pi]])
                    s.op(S, lambda e, mc=mc, pi=pi: e.activation(
                        out=mid_fm[:, mc * TTOK:(mc + 1) * TTOK], in_=PAt(pi),
                        func=AF.Silu, bias=b1_sb[:, mc:mc + 1]),
                        reads=[PK[pi], "C:b1"], writes=["A", f"mid{mc}"])
            for nd in range(2):
                for ig in range(4):
                    s.op(TE, lambda e, ig=ig, nd=nd: e.matmul(
                        PAt(ig), onesrow[0:1, :],
                        b2_sb[:, nd * 512:(nd + 1) * 512], start=True, stop=False),
                        reads=["C:b2", "C:orow"], writes=[PK[ig]])
                for sess in range(4):
                    hb = sess % 2
                    dma(SY, WB[hb][:, 0:4096], w2_sc[nd * 4 + sess],
                        reads=["D:w2"], writes=[f"WB{hb}"])
                    for ig in range(4):
                        for kc in range(8):
                            kcg = sess * 8 + kc
                            s.op(TE, lambda e, kc=kc, hb=hb, ig=ig, kcg=kcg: e.matmul(
                                PAt(ig),
                                mid_fm[:, kcg * TTOK + ig * 128: kcg * TTOK + (ig + 1) * 128],
                                WB[hb][:, kc * 512:(kc + 1) * 512],
                                start=False, stop=(kcg == 31)),
                                reads=[f"mid{kcg}", f"WB{hb}"], writes=[PK[ig]])
                for ig in range(4):
                    s.op(V, lambda e, ig=ig, nd=nd: e.scalar_tensor_tensor(
                        out=sg_tm[:, ig * D + nd * 512: ig * D + (nd + 1) * 512],
                        in0=PAt(ig), scalar=1.0,
                        in1=h_tm[:, ig * D + nd * 512: ig * D + (nd + 1) * 512],
                        op0=ALU.mult, op1=ALU.add),
                        reads=[PK[ig], "h"], writes=["E", "sgk"])
            tap("s_attn", sg_tm[:], ["sgk"], tile)

            for nd2 in range(2):
                hb = nd2
                dma(SY, WB[hb].rearrange("p (kc n) -> p kc n", n=512),
                    wc_sc.rearrange("kp (kc ch n) -> kp kc ch n", ch=2, n=512)[:, :, nd2, :],
                    reads=["D:wc"], writes=[f"WB{hb}"])
                for ig in range(4):
                    pi = (nd2 * 4 + ig) % 4
                    s.op(TE, lambda e, ig=ig, nd2=nd2, pi=pi: e.matmul(
                        PAt(pi), onesrow[32:33, :],
                        fb_sb[:, nd2 * 512:(nd2 + 1) * 512], start=True, stop=False),
                        reads=["C:fb", "C:orow"], writes=[PK[pi]])
                    for kc in range(8):
                        s.op(TE, lambda e, ig=ig, nd2=nd2, kc=kc, pi=pi, hb=hb: e.matmul(
                            PAt(pi),
                            xn_fm[:, kc * TTOK + ig * 128: kc * TTOK + (ig + 1) * 128],
                            WB[hb][:, kc * 512:(kc + 1) * 512],
                            start=False, stop=(kc == 7)),
                            reads=["xnfm", f"WB{hb}"], writes=[PK[pi]])
                    s.op(S, lambda e, ig=ig, nd2=nd2, pi=pi: e.activation(
                        out=g_tm[:, ig * D + nd2 * 512: ig * D + (nd2 + 1) * 512],
                        in_=PAt(pi), func=AF.Silu),
                        reads=[PK[pi]], writes=["B", "gk"])
            tap("g_tm", g_tm[:], ["gk"], tile)
            dma(SY, x2_tm.rearrange("p (g d) -> p g d", d=D),
                x_h[t0:t0 + TTOK, :].rearrange("(g p) d -> p g d", p=128),
                writes=["F"])
            for g in range(4):
                s.op(V, lambda e, g=g: e.tensor_mul(
                    out=sg_tm[:, g * D:(g + 1) * D], in0=sg_tm[:, g * D:(g + 1) * D],
                    in1=g_tm[:, g * D:(g + 1) * D]),
                    reads=["sgk", "gk", "B", "E"], writes=["E", "sgk"])
                s.op(V, lambda e, g=g: e.tensor_add(
                    out=out_tm[:, g * D:(g + 1) * D], in0=x2_tm[:, g * D:(g + 1) * D],
                    in1=sg_tm[:, g * D:(g + 1) * D]),
                    reads=["F", "sgk"], writes=["F", "outb"])
            dma(GP, out_h[t0:t0 + TTOK, :].rearrange("(g p) d -> p g d", p=128),
                out_tm.rearrange("p (g d) -> p g d", d=D),
                reads=["outb", "F"], writes=["outq"])

        s.op(GP, lambda e: e.engine_nop(), reads=["outq"])

        s.finalize(nc, ctx)
        with nc.Block() as block:
            s.replay(block)

    return nc


def _shard_inputs(inputs):
    maps = []
    f32 = lambda a: np.ascontiguousarray(np.asarray(a, dtype=np.float32))
    for b in range(8):
        maps.append({
            "x": f32(np.asarray(inputs["L0_attn_hidden_tokens"])[b]),
            "W_in": f32(inputs["L0_attn_W_in"]),
            "conv_w": f32(inputs["L0_attn_conv_w"]),
            "conv_b": f32(inputs["L0_attn_conv_b"]),
            "dt_bias": f32(inputs["L0_attn_dt_bias"]),
            "A_log": f32(inputs["L0_attn_A_log"]),
            "D_param": f32(inputs["L0_attn_D_param"]),
            "norm_w": f32(inputs["L0_attn_norm_w"]),
            "W_out": f32(inputs["L0_attn_W_out"]),
            "rms_w": f32(inputs["L0_attn_rms_w"]),
            "mlp_w1": f32(inputs["L0_attn_mlp_w1"]),
            "mlp_b1": f32(inputs["L0_attn_mlp_b1"]),
            "mlp_w2": f32(inputs["L0_attn_mlp_w2"]),
            "mlp_b2": f32(inputs["L0_attn_mlp_b2"]),
            "G1_R": f32(np.asarray(inputs["L0_ffn_G1_R"])[0]),
            "G2_R": f32(np.asarray(inputs["L0_ffn_G2_R"])[:, :, 0, :]),
            "G1_I": f32(np.asarray(inputs["L0_ffn_G1_I"])[0]),
            "G2_I": f32(np.asarray(inputs["L0_ffn_G2_I"])[:, :, 0, :]),
            "bias_real": f32(inputs["L0_ffn_bias_real"]),
            "bias_imag": f32(inputs["L0_ffn_bias_imag"]),
        })
    return maps


def kernel(**inputs):
    nc = build()
    in_maps = _shard_inputs(inputs)
    res = run_bass_kernel_spmd(nc, in_maps, core_ids=list(range(8)))
    return np.stack([r["out"] for r in res.results], axis=0)
